# version 03
# baseline (speedup 1.0000x reference)
import math
from contextlib import ExitStack

import numpy as np
import ml_dtypes

import concourse.bass as bass
import concourse.mybir as mybir
from concourse.bass import IndirectOffsetOnAxis
from concourse.bass_utils import run_bass_kernel_spmd

F32 = mybir.dt.float32
BF16 = mybir.dt.bfloat16
I32 = mybir.dt.int32
AF = mybir.ActivationFunctionType
ALU = mybir.AluOpType
NP_BF16 = ml_dtypes.bfloat16

NCORES = 8
D = 4096
TPC = 1024
H = 32
HD = 128
S = 4096
HPC = 8
EPS = 1e-5

ENGINES = ("sync", "tensor", "vector", "scalar", "gpsimd")
SBUF_TOP = 229344


class Sem:
    __slots__ = ("h", "count", "name")

    def __init__(self, h, name):
        self.h = h
        self.name = name
        self.count = 0


class Prog:
    def __init__(self, name="prog"):
        self.nc = bass.Bass("TRN2", target_bir_lowering=False)
        self.name = name
        self.ops = {e: [] for e in ENGINES}
        self._shared = {}
        self._es = None
        self._last_wait = {e: {} for e in ENGINES}
        self._arena_low = SBUF_TOP
        self._auto = {}
        self._uid = 0

    def uid(self):
        self._uid += 1
        return self._uid

    def dram(self, name, shape, dtype, kind="Internal"):
        return self.nc.dram_tensor(name, list(shape), dtype, kind=kind)

    def sbuf(self, name, shape, dtype):
        assert self._es is not None
        return self._es.enter_context(
            self.nc.sbuf_tensor(f"{name}_{self.uid()}", list(shape), dtype))

    def psum(self, name, shape, dtype=F32):
        assert self._es is not None
        return self._root_es.enter_context(
            self.nc.psum_tensor(f"{name}_{self.uid()}", list(shape), dtype))

    def sem(self, name):
        assert self._es is not None
        if not hasattr(self, "_sems"):
            self._sems = []
            self._sem_cache = {}
        if name in self._sem_cache:
            return self._sem_cache[name]
        h = self._root_es.enter_context(self.nc.semaphore(f"{name}_{self.uid()}"))
        s = Sem(h, name)
        self._sems.append(s)
        self._sem_cache[name] = s
        return s

    def scope(self):
        from contextlib import contextmanager

        @contextmanager
        def _cm():
            outer_es = self._es
            outer_shared = self._shared
            self._shared = {}
            with ExitStack() as es:
                self._es = es
                try:
                    yield
                finally:
                    self._es = outer_es
                    self._shared = outer_shared
        return _cm()

    def barrier(self):
        sv = [(s, s.count) for s in getattr(self, "_sems", []) if s.count > 0]
        for e in ENGINES:
            self.wait(e, *sv)

    def shared_sbuf(self, name, shape, dtype):
        key = ("sb", name)
        if key in self._shared:
            h, sh, dt = self._shared[key]
            assert list(sh) == list(shape) and dt == dtype, f"shared sbuf {name} mismatch"
            return h
        h = self.sbuf(name, shape, dtype)
        self._shared[key] = (h, list(shape), dtype)
        return h

    def at(self, name, shape, dtype, offset):
        self._arena_low = min(self._arena_low, offset)
        return self.nc.alloc_sbuf_tensor_at(f"{name}_{self.uid()}", list(shape), dtype,
                                            offset=offset)

    def shared_psum(self, name, shape, dtype=F32):
        if not hasattr(self, "_shared_ps"):
            self._shared_ps = {}
        if name in self._shared_ps:
            h, sh, dt = self._shared_ps[name]
            assert list(sh) == list(shape) and dt == dtype, f"shared psum {name} mismatch"
            return h
        h = self.psum(name, shape, dtype)
        self._shared_ps[name] = (h, list(shape), dtype)
        return h

    def _emit(self, eng, waits, fn, sem, inc, is_dma=False):
        lw = self._last_wait[eng]
        if not is_dma and eng in ("vector", "scalar", "gpsimd"):
            if sem is None:
                if eng not in self._auto:
                    self._auto[eng] = self.sem("auto_" + eng)
                sem = self._auto[eng]
                inc = 1
            prev = getattr(self, "_prev_op", {}).get(eng)
            if prev is not None:
                waits = list(waits) + [prev]
        filt = []
        for s, v in waits:
            if v <= 0:
                continue
            if lw.get(id(s), -1) >= v:
                continue
            lw[id(s)] = v
            filt.append((s.h, v))
        if sem is not None:
            sem.count += inc
            if not is_dma and eng in ("vector", "scalar", "gpsimd"):
                if not hasattr(self, "_prev_op"):
                    self._prev_op = {}
                self._prev_op[eng] = (sem, sem.count)

        def closure(E, fn=fn, filt=tuple(filt), semh=(sem.h if sem else None), inc=inc):
            for h, v in filt:
                E.wait_ge(h, v)
            inst = fn(E)
            if semh is not None:
                inst.then_inc(semh, inc)

        self.ops[eng].append(closure)
        return sem.count if sem else None

    def op(self, eng, fn, waits=(), sem=None, inc=1):
        return self._emit(eng, waits, fn, sem, inc)

    def dma(self, eng, out, in_, waits=(), sem=None, **kw):
        return self._emit(eng, waits, lambda E: E.dma_start(out=out, in_=in_, **kw),
                          sem, 16, is_dma=True)

    def idma(self, out, in_, idx_ap, waits=(), sem=None, eoff=0):
        return self._emit("gpsimd", waits,
                          lambda E: E.indirect_dma_start(
                              out, None, in_, IndirectOffsetOnAxis(idx_ap, 0),
                              element_offset=eoff),
                          sem, 16, is_dma=True)

    def dma_tr(self, eng, out, in_, waits=(), sem=None):
        return self._emit(eng, waits, lambda E: E.dma_start_transpose(out, in_),
                          sem, 16, is_dma=True)

    def wait(self, eng, *sems_vals):
        lw = self._last_wait[eng]
        filt = []
        for s, v in sems_vals:
            if v <= 0 or lw.get(id(s), -1) >= v:
                continue
            lw[id(s)] = v
            filt.append((s.h, v))
        if not filt:
            return

        def closure(E, filt=tuple(filt)):
            for h, v in filt:
                E.wait_ge(h, v)

        self.ops[eng].append(closure)

    def mm(self, out, lhsT, rhs, start, stop, waits=(), sem=None):
        return self._emit(
            "tensor", waits,
            lambda E: E.matmul(out, lhsT, rhs, start=start, stop=stop), sem, 1)

    def build(self, emit_fn):
        with ExitStack() as es:
            self._es = es
            self._root_es = es
            emit_fn(self)
            assert self.nc.sbuf_bytes_remaining >= SBUF_TOP - self._arena_low, (
                f"bump allocations overlap arena: remaining "
                f"{self.nc.sbuf_bytes_remaining} < arena {SBUF_TOP - self._arena_low}")
            with self.nc.Block() as block:
                for e in ENGINES:
                    lst = self.ops[e]
                    if not lst:
                        continue
                    getattr(block, e)(lambda E, lst=lst: [f(E) for f in lst])
        self._es = None
        return self.nc



def emit_ln_stats(p: Prog, x_ap, rowbuf, tag, want_var, hi_lo, nbuf=2):
    s_x = [p.sem(f"sxl{i}") for i in range(nbuf)]
    s_dv = p.sem("ls_dv"); s_mm = p.sem("ls_mm")
    s_sq = p.sem("ls_sq"); s_row = p.sem("ls_row")
    s_row2 = p.sem("ls_rw2"); s_g = p.sem("ls_g")

    xf32 = [p.shared_sbuf(f"xp{i}", [128, TPC], F32) for i in range(nbuf)]
    xhi = [p.shared_sbuf(f"xh{i}", [128, TPC], BF16) for i in range(nbuf)]
    xlo = [p.shared_sbuf(f"xl{i}", [128, TPC], BF16) for i in range(nbuf)] if hi_lo else None
    xsq = [p.shared_sbuf(f"xl{i}", [128, TPC], BF16) for i in range(nbuf)] if want_var else None
    ones128 = p.shared_sbuf("ones128", [128, 1], BF16)
    mean_row = p.sbuf(f"{tag}_mrow", [1, TPC], F32)
    mean_b = p.shared_sbuf("mean_b", [128, TPC], F32)
    ps_mean = p.shared_psum("prowA", [1, TPC])
    if want_var:
        rstd_row = p.sbuf(f"{tag}_rrow", [1, TPC], F32)
        sd_row = p.sbuf(f"{tag}_sdrow", [1, TPC], F32)
        rstd_b = p.shared_sbuf("rstd_b", [128, TPC], F32)
        ps_sq = p.shared_psum("prowB", [1, TPC])

    s_ms = p.sem("ls_ms")
    dv0, mm0, sq0 = s_dv.count, s_mm.count, s_sq.count
    p.op("gpsimd", lambda E: E.memset(ones128[:], 1.0), sem=s_ms)
    ms_v = s_ms.count
    n_dve = (2 if hi_lo else 1)
    mm_per = (2 if hi_lo else 1) * 2 + (2 if want_var else 0)
    for i in range(32):
        b = i % nbuf
        iw = i - nbuf + 1
        xv = p.dma("scalar", xf32[b][:], x_ap[i * 128:(i + 1) * 128, :],
                   waits=[(s_dv, dv0 + n_dve * iw)], sem=s_x[b])
        p.op("vector", lambda E, b=b: E.tensor_copy(xhi[b][:], xf32[b][:]),
             waits=[(s_x[b], xv), (s_mm, mm0 + max(0, mm_per * iw)),
                    (s_sq, (sq0 + max(0, iw)) if want_var else 0)], sem=s_dv)
        if hi_lo:
            p.op("vector", lambda E, b=b: E.tensor_tensor(
                xlo[b][:], xf32[b][:], xhi[b][:], ALU.subtract), sem=s_dv)
        if want_var:
            p.op("scalar", lambda E, b=b: E.activation(xsq[b][:], xhi[b][:], AF.Square),
                 waits=[(s_dv, dv0 + n_dve * i + 1)], sem=s_sq)
        for half in range(2):
            sl = slice(half * 512, (half + 1) * 512)
            p.mm(ps_mean[0:1, sl], ones128[:], xhi[b][:, sl],
                 start=(i == 0), stop=(not hi_lo and i == 31),
                 waits=[(s_dv, dv0 + n_dve * i + n_dve), (s_ms, ms_v),
                        (s_sq, (sq0 + i + 1) if want_var else 0)], sem=s_mm)
            if hi_lo:
                p.mm(ps_mean[0:1, sl], ones128[:], xlo[b][:, sl],
                     start=False, stop=(i == 31), sem=s_mm)
            if want_var:
                p.mm(ps_sq[0:1, sl], ones128[:], xsq[b][:, sl],
                     start=(i == 0), stop=(i == 31), sem=s_mm)
    mm_end = s_mm.count
    p.op("vector", lambda E: E.tensor_scalar_mul(mean_row[:], ps_mean[0:1, :], 1.0 / D),
         waits=[(s_mm, mm_end)], sem=s_dv)
    rv = p.dma("sync", rowbuf[0:1, :], mean_row[:], waits=[(s_dv, s_dv.count)], sem=s_row)
    p.dma("gpsimd", mean_b[:], rowbuf[0:1, :].broadcast_to((128, TPC)),
          waits=[(s_row, rv)], sem=s_g)
    if want_var:
        p.op("vector", lambda E: E.tensor_scalar_mul(sd_row[:], ps_sq[0:1, :], 1.0 / D),
             waits=[(s_mm, mm_end)], sem=s_dv)
        p.op("vector", lambda E: E.tensor_tensor(rstd_row[:], mean_row[:], mean_row[:], ALU.mult),
             sem=s_dv)
        p.op("vector", lambda E: E.tensor_tensor(sd_row[:], sd_row[:], rstd_row[:], ALU.subtract),
             sem=s_dv)
        p.op("scalar", lambda E: E.activation(sd_row[:], sd_row[:], AF.Ln, bias=np.float32(EPS)),
             waits=[(s_dv, s_dv.count)], sem=s_sq)
        p.op("scalar", lambda E: E.activation(rstd_row[:], sd_row[:], AF.Exp, scale=np.float32(-0.5)),
             sem=s_sq)
        rv2 = p.dma("sync", rowbuf[1:2, :], rstd_row[:], waits=[(s_sq, s_sq.count)], sem=s_row2)
        p.dma("gpsimd", rstd_b[:], rowbuf[1:2, :].broadcast_to((128, TPC)),
              waits=[(s_row2, rv2)], sem=s_g)
    done = (s_g, s_g.count)
    return dict(mean_b=mean_b, rstd_b=(rstd_b if want_var else None), done=done,
                s_dv=s_dv)


def emit_xln(p: Prog, x_ap, st, tag, sign=False):
    s_x2 = [p.sem(f"xlx{i}") for i in range(2)]
    s_dv = p.sem("xl_dv"); s_o = p.sem("xl_o")
    dv0, o0 = s_dv.count, s_o.count
    xf32 = [p.shared_sbuf(f"xp{i}", [128, TPC], F32) for i in range(2)]
    tsub = [p.shared_sbuf(f"tsb{i}", [128, TPC], F32) for i in range(2)]
    act_bf = p.shared_sbuf("act_bf", [128, 32 * TPC], BF16)
    mean_b = st["mean_b"]
    for i in range(32):
        b = i % 2
        xv = p.dma("scalar", xf32[b][:], x_ap[i * 128:(i + 1) * 128, :],
                   waits=[(s_dv, dv0 + i - 1), (st["s_dv"], st["s_dv"].count)], sem=s_x2[b])
        p.op("vector", lambda E, b=b: E.tensor_tensor(tsub[b][:], xf32[b][:], mean_b[:], ALU.subtract),
             waits=[(s_x2[b], xv), st["done"], (s_o, o0 + i - 1)], sem=s_dv)
        if sign:
            p.op("scalar", lambda E, b=b, i=i: E.activation(
                act_bf[:, i * TPC:(i + 1) * TPC], tsub[b][:], AF.Sign),
                waits=[(s_dv, s_dv.count)], sem=s_o)
        else:
            p.op("gpsimd", lambda E, b=b, i=i: E.tensor_tensor(
                act_bf[:, i * TPC:(i + 1) * TPC], tsub[b][:], st["rstd_b"][:], ALU.mult),
                waits=[(s_dv, s_dv.count)], sem=s_o)
    return act_bf, s_o, o0


def emit_ffn0(p: Prog, x_ap, w0t_ap, alpha_ap, bias_ap, y_ap, sw0, rowbuf, tag):
    t = tag
    s_wld = [p.sem(f"gwl{i}") for i in range(2)]
    s_sg = p.sem(t + "sg")
    s_sw = [p.sem(t + f"sw{i}") for i in range(2)]
    s_wld2 = [p.sem(f"swl{i}") for i in range(2)]
    s_mm2 = p.sem(t + "mm2"); s_ev = p.sem(t + "ev")
    s_x3 = [p.sem(f"sxl{i}") for i in range(2)]
    s_dv2 = p.sem(t + "dv2")
    s_st = [p.sem(f"yst{i}") for i in range(2)]
    s_ab = p.sem(t + "ab"); s_abd = p.sem(t + "abd")

    wslab = [p.shared_sbuf(f"wsl{i}", [128, D], BF16) for i in range(2)]
    wsign = [p.shared_sbuf(f"wsg{i}", [128, D], BF16) for i in range(2)]
    wbuf = [p.shared_sbuf(f"wb{i}", [128, 32 * 512], BF16) for i in range(2)]
    ab_sb = p.sbuf(f"{t}_ab", [128, 64], F32)
    evtmp = [p.shared_sbuf(f"ev{i}", [128, 512], F32) for i in range(2)]
    xres = [p.shared_sbuf(f"xr{i}", [128, 512], F32) for i in range(2)]
    yt = [p.shared_sbuf(f"yt{i}", [128, 512], F32) for i in range(2)]
    pp = [p.shared_psum(f"pp{i}", [128, 512]) for i in range(2)]

    sg0, mm20, ev0, dv20 = s_sg.count, s_mm2.count, s_ev.count, s_dv2.count
    p.dma("scalar", ab_sb[:, 0:32], alpha_ap, sem=s_ab)
    ab_v = p.dma("scalar", ab_sb[:, 32:64], bias_ap, sem=s_ab)
    p.op("vector", lambda E: E.tensor_scalar_mul(ab_sb[:, 0:64], ab_sb[:, 0:64], 0.5),
         waits=[(s_ab, ab_v)], sem=s_abd)
    abd_v = s_abd.count

    sw_v = {}
    for i in range(32):
        b = i % 2
        wv_ = p.dma("gpsimd", wslab[b][:], w0t_ap[i * 128:(i + 1) * 128, :],
                    waits=[(s_sg, sg0 + i - 1)], sem=s_wld[b])
        p.op("scalar", lambda E, b=b: E.activation(wsign[b][:], wslab[b][:], AF.Sign),
             waits=[(s_wld[b], wv_), (s_sw[b], sw_v.get(i - 2, 0))],
             sem=s_sg)
        sw_v[i] = p.dma("sync", sw0[i * 128:(i + 1) * 128, :], wsign[b][:],
                        waits=[(s_sg, sg0 + i + 1)], sem=s_sw[b])

    st = emit_ln_stats(p, x_ap, rowbuf, t, want_var=False, hi_lo=True)
    act_bf, s_act, act_base = emit_xln(p, x_ap, st, t, sign=True)

    for oslab in range(8):
        b = oslab % 2
        for k in range(32):
            wl_v = p.dma("sync", wbuf[b][:, k * 512:(k + 1) * 512],
                         sw0[k * 128:(k + 1) * 128, oslab * 512:(oslab + 1) * 512],
                         waits=[(s_sw[k % 2], sw_v[k]),
                                (s_mm2, mm20 + max(0, (oslab - 1) * 256))],
                         sem=s_wld2[b])
        for osub in range(4):
            o = oslab * 4 + osub
            for tb in range(2):
                u = o * 2 + tb
                pb = u % 2
                for k in range(32):
                    p.mm(pp[pb][:], wbuf[b][:, k * 512 + osub * 128: k * 512 + osub * 128 + 128],
                         act_bf[:, k * TPC + tb * 512: k * TPC + tb * 512 + 512],
                         start=(k == 0), stop=(k == 31),
                         waits=[(s_wld2[b], wl_v),
                                (s_act, act_base + k + 1),
                                (s_ev, ev0 + max(0, u - 1))],
                         sem=s_mm2)
                p.op("vector", lambda E, pb=pb, o=o: E.tensor_scalar(
                    evtmp[pb][:], pp[pb][:], ab_sb[:, o:o + 1], ab_sb[:, 32 + o:32 + o + 1],
                    ALU.mult, ALU.add),
                    waits=[(s_mm2, s_mm2.count), (s_dv2, dv20 + max(0, u - 1)),
                           (s_abd, abd_v)],
                    sem=s_ev)
                xr_v = p.dma("scalar", xres[pb][:], x_ap[o * 128:(o + 1) * 128, tb * 512:(tb + 1) * 512],
                             waits=[(s_dv2, dv20 + max(0, u - 1)),
                                    (s_act, act_base + 32)], sem=s_x3[pb])
                p.op("gpsimd", lambda E, pb=pb: E.tensor_tensor(
                    yt[pb][:], evtmp[pb][:], xres[pb][:], ALU.add),
                    waits=[(s_ev, s_ev.count), (s_x3[pb], xr_v),
                           (s_st[pb], s_st[pb].count)],
                    sem=s_dv2)
                p.dma("sync", y_ap[o * 128:(o + 1) * 128, tb * 512:(tb + 1) * 512],
                      yt[pb][:], waits=[(s_dv2, s_dv2.count)], sem=s_st[pb])
    p.wait("sync", (s_st[0], s_st[0].count), (s_st[1], s_st[1].count))
    return s_st[0]


def emit_qkv(p: Prog, x_ap, weights, y_outs, rowbuf, tag, layer,
             rot_tabs=None, battn_ap=None):
    t = tag
    st = emit_ln_stats(p, x_ap, rowbuf, t, want_var=True, hi_lo=False)
    act_bf, s_act, act_base = emit_xln(p, x_ap, st, t, sign=False)

    s_wld = [p.sem(f"gwl{i}") for i in range(2)]
    s_mm = p.sem(t + "mm"); s_ev = p.sem(t + "ev")
    s_sq = p.sem(t + "sq"); s_nm = p.sem(t + "nm"); s_nr = p.sem(t + "nr")
    s_st = [p.sem(t + f"st{i}") for i in range(4)]
    s_tab = p.sem(t + "tab"); s_g = p.sem(t + "g")
    s_tr = p.sem(t + "tr"); s_vst = p.sem(t + "vst")
    s_n2 = [p.sem(t + f"n2{i}") for i in range(2)]

    wbuf = [p.shared_sbuf(f"wb{i}", [128, 32 * 512], BF16) for i in range(2)]
    qk_t = [p.shared_sbuf(f"qk{i}", [128, 512], BF16) for i in range(4)]
    rot = [p.shared_sbuf("rot0", [128, 512], F32)]
    ta = [p.shared_sbuf("ta0", [128, 512], F32)]
    tb_ = [p.shared_sbuf("tb0", [128, 512], F32)]
    sqt = [p.shared_sbuf(f"sq{i}", [128, 512], BF16) for i in range(2)]
    n2r = [p.sbuf(f"{t}_n2r{i}", [1, 512], BF16) for i in range(2)]
    ones128 = p.shared_sbuf("ones128", [128, 1], BF16)
    pp = [p.shared_psum(f"pp{i}", [128, 512]) for i in range(2)]
    ps_n = p.shared_psum("prowB", [1, TPC])[0:1, 0:512]
    if rot_tabs is not None:
        cosq = p.sbuf(f"{t}_cq", [128, TPC], BF16)
        sinq = p.sbuf(f"{t}_sn", [128, TPC], BF16)
        cosk = p.sbuf(f"{t}_ck", [128, TPC], BF16)
        sink = p.sbuf(f"{t}_sk", [128, TPC], BF16)
        for sb, apn in zip((cosq, sinq, cosk, sink), rot_tabs):
            p.dma("gpsimd", sb[:], apn, sem=s_tab)
    if battn_ap is not None:
        nch = sum(len(r[1]) for r in weights)
        bat = p.sbuf(f"{t}_bat", [128, nch], F32)
        p.dma("gpsimd", bat[:], battn_ap, sem=s_tab)
    tab_done = (s_tab, s_tab.count)

    u = 0
    ru = 0
    nunit = 0
    mm0, ev0, g0 = s_mm.count, s_ev.count, s_g.count
    nm0, nr0 = s_nm.count, s_nr.count
    slab_mm_end = {}
    slab_id = 0
    ev_after_unit = {}
    for w_ap, roles in weights:
        nslab = len(roles) // 4
        for oslab in range(nslab):
            b = slab_id % 2
            for k in range(32):
                wl_v = p.dma("gpsimd", wbuf[b][:, k * 512:(k + 1) * 512],
                             w_ap[k * 128:(k + 1) * 128, oslab * 512:(oslab + 1) * 512],
                             waits=[(s_mm, slab_mm_end.get(slab_id - 2, mm0))],
                             sem=s_wld[b])
            for osub in range(4):
                oc = oslab * 4 + osub
                role, head = roles[oc]
                gc = slab_id * 4 + osub
                for tbk in range(2):
                    pb = u % 2
                    for k in range(32):
                        p.mm(pp[pb][:], wbuf[b][:, k * 512 + osub * 128: k * 512 + osub * 128 + 128],
                             act_bf[:, k * TPC + tbk * 512: k * TPC + tbk * 512 + 512],
                             start=(k == 0), stop=(k == 31),
                             waits=[(s_wld[b], wl_v),
                                    (s_act, act_base + k + 1),
                                    (s_ev, ev_after_unit.get(u - 2, ev0))],
                             sem=s_mm)
                    qb = u % 4
                    war_store = (s_st[qb], s_st[qb].count)
                    if rot_tabs is not None and role in ("q", "k"):
                        cs, sn = (cosq, sinq) if role == "q" else (cosk, sink)
                        rb = 0
                        csl = slice(tbk * 512, (tbk + 1) * 512)
                        p.op("vector", lambda E, rb=rb, pb=pb: E.tensor_copy(
                            rot[rb][0:64, :], pp[pb][64:128, :]),
                            waits=[(s_mm, s_mm.count), tab_done,
                                   (s_g, g0 + ru)], sem=s_ev)
                        p.op("vector", lambda E, rb=rb, pb=pb: E.tensor_copy(
                            rot[rb][64:128, :], pp[pb][0:64, :]), sem=s_ev)
                        p.op("vector", lambda E, rb=rb, pb=pb, cs=cs, csl=csl: E.tensor_tensor(
                            ta[rb][:], pp[pb][:], cs[:, csl], ALU.mult),
                            waits=[(s_ev, ev_after_unit.get(u - 1, ev0))], sem=s_ev)
                        p.op("gpsimd", lambda E, rb=rb, sn=sn, csl=csl: E.tensor_tensor(
                            tb_[rb][:], rot[rb][:], sn[:, csl], ALU.mult),
                            waits=[(s_ev, s_ev.count)], sem=s_g)
                        p.op("vector", lambda E, rb=rb, qb=qb: E.tensor_tensor(
                            qk_t[qb][:], ta[rb][:], tb_[rb][:], ALU.add),
                            waits=[(s_g, s_g.count), war_store], sem=s_ev)
                        ru += 1
                    elif role in ("q", "k"):
                        if role == "q":
                            p.op("vector", lambda E, pb=pb, qb=qb, gc=gc: E.tensor_scalar(
                                qk_t[qb][:], pp[pb][:], bat[:, gc:gc + 1], 1.0 / 128.0,
                                ALU.add, ALU.mult),
                                waits=[(s_mm, s_mm.count), tab_done, war_store], sem=s_ev)
                        else:
                            p.op("vector", lambda E, pb=pb, qb=qb, gc=gc: E.tensor_scalar(
                                qk_t[qb][:], pp[pb][:], bat[:, gc:gc + 1], None, ALU.add),
                                waits=[(s_mm, s_mm.count), tab_done, war_store], sem=s_ev)
                    else:
                        if battn_ap is not None:
                            p.op("vector", lambda E, pb=pb, qb=qb, gc=gc: E.tensor_scalar(
                                qk_t[qb][:], pp[pb][:], bat[:, gc:gc + 1], None, ALU.add),
                                waits=[(s_mm, s_mm.count), tab_done, war_store], sem=s_ev)
                        else:
                            p.op("vector", lambda E, pb=pb, qb=qb: E.tensor_copy(
                                qk_t[qb][:], pp[pb][:]),
                                waits=[(s_mm, s_mm.count), war_store], sem=s_ev)
                    ev_after_unit[u] = s_ev.count
                    dst = y_outs["v"] if role == "v" else y_outs[role]
                    p.dma("sync", dst[head * 128:(head + 1) * 128, tbk * 512:(tbk + 1) * 512],
                          qk_t[qb][:], waits=[(s_ev, s_ev.count)], sem=s_st[qb])
                    if role in ("q", "k"):
                        sb2 = nunit % 2
                        p.op("scalar", lambda E, sb2=sb2, qb=qb: E.activation(
                            sqt[sb2][:], qk_t[qb][:], AF.Square),
                            waits=[(s_ev, s_ev.count), (s_nm, nm0 + max(0, nunit - 1))], sem=s_sq)
                        p.mm(ps_n, ones128[:], sqt[sb2][:], start=True, stop=True,
                             waits=[(s_sq, s_sq.count), (s_nr, nr0 + nunit),
                                    (st["s_dv"], st["s_dv"].count)], sem=s_nm)
                        p.op("vector", lambda E, sb2=sb2: E.tensor_copy(
                            n2r[sb2][:], ps_n),
                            waits=[(s_nm, s_nm.count), (s_n2[sb2], s_n2[sb2].count)],
                            sem=s_nr)
                        row = head if role == "q" else 32 + head
                        p.dma("sync", y_outs["n2"][row:row + 1, tbk * 512:(tbk + 1) * 512],
                              n2r[sb2][:], waits=[(s_nr, s_nr.count)], sem=s_n2[sb2])
                        nunit += 1
                    u += 1
            slab_mm_end[slab_id] = s_mm.count
            slab_id += 1

    vtt = p.shared_sbuf("wsl0", [128, D], BF16)
    all_st = [(s, s.count) for s in s_st]
    for tt in range(8):
        tr_v = p.dma_tr("sync", vtt[:], y_outs["v"][:, tt * 128:(tt + 1) * 128],
                        waits=all_st + [(s_vst, s_vst.count)], sem=s_tr)
        p.dma("scalar", y_outs["vt"][tt * 128:(tt + 1) * 128, :], vtt[:],
              waits=[(s_tr, tr_v)], sem=s_vst)
    p.wait("sync", (s_vst, s_vst.count), (s_n2[0], s_n2[0].count),
           (s_n2[1], s_n2[1].count), *[(s, s.count) for s in s_st])
    return s_vst


def emit_attn(p: Prog, qh_ap, kh_ap, vh_ap, n2_ap, masks_ap, o_ap, brow_dram,
              zrow_dram, tag, layer, alt_ap=None, qsl_ap=None, ones2_ap=None,
              gath=None):
    t = tag
    s_ld = p.sem(t + "ld"); s_b = p.sem(t + "b"); s_bs = p.sem(t + "bs")
    s_mm = p.sem(t + "mm"); s_ex = p.sem(t + "ex"); s_mk = p.sem(t + "mk")
    s_q = [p.sem(t + f"q{i}") for i in range(2)]
    s_al = p.sem(t + "al")
    s_alb = [p.sem(t + f"alb{i}") for i in range(2)]
    s_ep = p.sem(t + "ep"); s_zr = p.sem(t + "zr")
    s_zst = [p.sem(t + f"zs{i}") for i in range(2)]
    s_zbb = [p.sem(t + f"zb{i}") for i in range(2)]
    s_st = [p.sem(t + f"st{i}") for i in range(2)]
    s_g = p.sem(t + "g")

    kh_sb = p.shared_sbuf("act_bf", [128, 32 * TPC], BF16)
    vh_sb0 = p.shared_sbuf("wb0", [128, 32 * 512], BF16)
    vh_sb1 = p.shared_sbuf("wb1", [128, 32 * 512], BF16)
    qwin = [p.sbuf(f"{t}_qw{i}", [128, 512], BF16) for i in range(2)]
    pt = [p.sbuf(f"{t}_pt{i}", [128, 512], BF16) for i in range(2)]
    masks = p.sbuf(f"{t}_mk", [128, 2048], BF16)
    n2sb = p.sbuf(f"{t}_n2", [16, S], BF16)
    n2max = p.sbuf(f"{t}_n2m", [16, 1], F32)
    brow = p.sbuf(f"{t}_br", [1, 16], F32)
    negb = p.sbuf(f"{t}_nb", [1, 8], F32)
    negb_cols = p.sbuf(f"{t}_nbc", [128, 8], F32)
    zr = p.sbuf(f"{t}_zr", [1, 512], F32)
    zb = [p.sbuf(f"{t}_zb{i}", [128, 512], F32) for i in range(2)]
    ot = [p.sbuf(f"{t}_ot{i}", [128, 512], BF16) for i in range(2)]
    ones128 = p.shared_sbuf("ones128", [128, 1], BF16)
    ps = [p.shared_psum(f"pp{i}", [128, 512]) for i in range(2)]
    po = [p.shared_psum(f"po{i}", [128, 512]) for i in range(2)]
    prowA = p.shared_psum("prowA", [1, TPC])
    pz = [prowA[0:1, 0:512], prowA[0:1, 512:1024]]
    if layer == 1:
        albuf = [p.sbuf(f"{t}_al{i}", [3, 4096], BF16) for i in range(2)]
        rhs3 = p.sbuf(f"{t}_r3", [3, 4096], BF16)
        qsl = p.sbuf(f"{t}_qs", [1, 4096], BF16)

    mm0, ex0, ep0 = s_mm.count, s_ex.count, s_ep.count
    g_v = p.op("gpsimd", lambda E: E.memset(ones128[:], 1.0), sem=s_g)
    if gath is not None:
        gap, idx_dram, h0col = gath
        s_ix = p.sem("ixs")
        idx_sb = p.sbuf(f"{t}_idx", [128, 132], I32)
        ix_v = p.dma("sync", idx_sb[:], idx_dram, sem=s_ix)
        ixw = (s_ix, ix_v)
    for h in range(8):
        if gath is None:
            ld_v = p.dma("sync", kh_sb[:, h * S:(h + 1) * S],
                         kh_ap[h * 128:(h + 1) * 128, :], sem=s_ld)
        else:
            for r in range(4):
                ld_v = p.idma(kh_sb[:, h * S + r * 1024: h * S + r * 1024 + 1024],
                              gap, idx_sb[:, h * 4 + r: h * 4 + r + 1],
                              waits=[ixw], sem=s_ld)
    for kt in range(32):
        dst = vh_sb0 if kt < 16 else vh_sb1
        co = (kt % 16) * 1024
        if gath is None:
            ld_v = p.dma("sync", dst[:, co:co + 1024],
                         vh_ap[kt * 128:(kt + 1) * 128, :], sem=s_ld)
        else:
            ld_v = p.idma(dst[:, co:co + 1024], gap,
                          idx_sb[:, 32 + kt:33 + kt], waits=[ixw], sem=s_ld)
    s_ldm = p.sem(t + "ldm")
    ldm_v = p.dma("gpsimd", masks[:], masks_ap, sem=s_ldm)
    s_bh = p.sem(t + "bh")
    if gath is None:
        b_v = p.dma("gpsimd", n2sb[:], n2_ap, sem=s_b)
    else:
        for r in range(4):
            b_v = p.idma(n2sb[0:16, r * 1024:(r + 1) * 1024], gap,
                         idx_sb[0:16, 64 + r:65 + r], waits=[ixw], sem=s_b)
    rx_v = p.op("vector", lambda E: E.reduce_max(n2max[:], n2sb[:], axis=mybir.AxisListType.X),
                waits=[(s_b, b_v)], sem=s_bs)
    bh_v = p.dma("sync", brow_dram[0:1, :].rearrange("a (b c) -> (a b) c", c=1),
                 n2max[:], waits=[(s_bs, rx_v)], sem=s_bh)
    bh_v = p.dma("sync", brow[:], brow_dram[0:1, :], waits=[(s_bh, bh_v)], sem=s_bh)
    p.op("vector", lambda E: E.tensor_tensor(negb[:], brow[0:1, 0:8], brow[0:1, 8:16], ALU.mult),
         waits=[(s_bh, bh_v)], sem=s_bs)
    p.op("scalar", lambda E: E.activation(negb[:], negb[:], AF.Sqrt),
         waits=[(s_bs, s_bs.count)], sem=s_bs)
    p.op("vector", lambda E: E.tensor_scalar_mul(negb[:], negb[:], -1.02),
         waits=[(s_bs, s_bs.count)], sem=s_bs)
    nb_done = s_bs.count
    if layer == 0:
        bh_v = p.dma("sync", brow_dram[0:1, 0:8], negb[:], waits=[(s_bs, nb_done)], sem=s_bh)
        p.dma("gpsimd", negb_cols[:], brow_dram[0:1, 0:8].broadcast_to((128, 8)),
              waits=[(s_bh, bh_v)], sem=s_b)
        bias_done = (s_b, s_b.count)
    else:
        p.dma("gpsimd", qsl[:], qsl_ap, sem=s_al)
        al_v0 = p.dma("gpsimd", rhs3[1:3, :], ones2_ap, sem=s_al)
        for h in range(8):
            p.op("vector", lambda E, h=h: E.tensor_scalar(
                rhs3[0:1, h * 512:(h + 1) * 512], qsl[0:1, h * 512:(h + 1) * 512],
                negb[0:1, h:h + 1], None, ALU.add),
                waits=[(s_al, al_v0), (s_bs, nb_done)], sem=s_bs)
        bias_done = (s_bs, s_bs.count)

    u = 0
    ep_after = {}
    pv_after_pt = {}
    alt_base = 0
    for h in range(8):
        for qb in range(8):
            nkt = 4 * (qb + 1)
            qi = h * 8 + qb
            pob = qi % 2
            qw_war = [(s_mm, mm0 if qi < 2 else pv_after_pt.get(("qdone", qi - 2), mm0))]
            if gath is None:
                q_v = p.dma("sync", qwin[qi % 2][:],
                            qh_ap[h * 128:(h + 1) * 128, qb * 512:(qb + 1) * 512],
                            waits=qw_war, sem=s_q[qi % 2])
            else:
                q_v = p.idma(qwin[qi % 2][:], gap,
                             idx_sb[:, 68 + h * 8 + qb: 69 + h * 8 + qb],
                             waits=qw_war + [ixw], sem=s_q[qi % 2],
                             eoff=(qb % 2) * 512)
            if layer == 1:
                al_v = p.dma("sync", albuf[qi % 2][:, 0:nkt * 128],
                             alt_ap[:, alt_base * 128:(alt_base + nkt) * 128],
                             waits=[(s_mm, mm0 if qi < 2 else pv_after_pt.get(("qdone", qi - 2), mm0))],
                             sem=s_alb[qi % 2])
            for kt in range(nkt):
                pb = u % 2
                mmw = [(s_ld, ld_v), (s_q[qi % 2], q_v),
                       (s_ex, ex0 + max(0, u - 1))]
                if layer == 1:
                    p.mm(ps[pb][:], albuf[qi % 2][:, kt * 128:kt * 128 + 128],
                         rhs3[:, h * 512:(h + 1) * 512],
                         start=True, stop=False,
                         waits=mmw + [(s_alb[qi % 2], al_v),
                                      bias_done],
                         sem=s_mm)
                    p.mm(ps[pb][:], kh_sb[:, h * S + kt * 128: h * S + kt * 128 + 128],
                         qwin[qi % 2][:, :], start=False, stop=True, sem=s_mm)
                else:
                    p.mm(ps[pb][:], kh_sb[:, h * S + kt * 128: h * S + kt * 128 + 128],
                         qwin[qi % 2][:, :], start=True, stop=True,
                         waits=mmw, sem=s_mm)
                smm = s_mm.count
                is_diag = (kt >= 4 * qb)
                if is_diag:
                    c = kt - 4 * qb
                    p.op("vector", lambda E, pb=pb, c=c: E.tensor_tensor(
                        ps[pb][:], ps[pb][:], masks[:, c * 512:(c + 1) * 512], ALU.add),
                        waits=[(s_mm, smm), (s_ldm, ldm_v)], sem=s_mk)
                if layer == 0:
                    p.op("scalar", lambda E, pb=pb, h=h: E.activation(
                        pt[pb][:], ps[pb][:], AF.Exp, bias=negb_cols[:, h:h + 1]),
                        waits=[(s_mm, smm), (s_mk, s_mk.count), bias_done,
                               (s_mm, pv_after_pt.get(u - 2, mm0))],
                        sem=s_ex)
                else:
                    p.op("scalar", lambda E, pb=pb: E.activation(
                        pt[pb][:], ps[pb][:], AF.Exp),
                        waits=[(s_mm, smm), (s_mk, s_mk.count),
                               (s_mm, pv_after_pt.get(u - 2, mm0))],
                        sem=s_ex)
                vsl = (vh_sb0 if kt < 16 else vh_sb1)
                vco = (kt % 16) * 1024 + h * 128
                p.mm(po[pob][:], vsl[:, vco:vco + 128], pt[pb][:],
                     start=(kt == 0), stop=(kt == nkt - 1),
                     waits=[(s_ex, s_ex.count),
                            (s_ep, ep_after.get(qi - 2, ep0)),
                            (s_zr, s_zr.count)],
                     sem=s_mm)
                p.mm(pz[pob][:], ones128[:], pt[pb][:],
                     start=(kt == 0), stop=(kt == nkt - 1), sem=s_mm)
                pv_after_pt[u] = s_mm.count
                u += 1
            pv_after_pt[("qdone", qi)] = s_mm.count
            alt_base += nkt
            p.op("vector", lambda E, pob=pob: E.reciprocal(zr[:], pz[pob][0:1, :]),
                 waits=[(s_mm, s_mm.count),
                        (s_zst[1 - pob], s_zst[1 - pob].count)], sem=s_zr)
            zs_v = p.dma("sync", zrow_dram[pob:pob + 1, :], zr[:],
                         waits=[(s_zr, s_zr.count),
                                (s_zbb[pob], s_zbb[pob].count)], sem=s_zst[pob])
            zb_v = p.dma("gpsimd", zb[pob][:],
                         zrow_dram[pob:pob + 1, :].broadcast_to((128, 512)),
                         waits=[(s_zst[pob], zs_v), (s_ep, ep_after.get(qi - 2, ep0))],
                         sem=s_zbb[pob])
            p.op("vector", lambda E, pob=pob: E.tensor_tensor(
                ot[pob][:], po[pob][:], zb[pob][:], ALU.mult),
                waits=[(s_zbb[pob], zb_v), (s_st[pob], s_st[pob].count)], sem=s_ep)
            ep_after[qi] = s_ep.count
            p.dma("sync", o_ap[h * 128:(h + 1) * 128, qb * 512:(qb + 1) * 512],
                  ot[pob][:], waits=[(s_ep, s_ep.count)], sem=s_st[pob])
    p.wait("sync", (s_st[0], s_st[0].count), (s_st[1], s_st[1].count))
    return s_st[0]



def emit_dequant(p: Prog, wspecs, tag):
    t = tag
    s_q = [p.sem(f"swl{i}") for i in range(2)]
    s_s = [p.sem(f"gwl{i}") for i in range(2)]
    s_dv = p.sem(t + "dv")
    s_w = [p.sem(f"yst{i}") for i in range(2)]
    s_z = p.sem(t + "z")
    s_zg = p.sem(t + "zg")
    NMAX = max(w["N"] for w in wspecs)
    N8MAX = NMAX // 8
    qt = [p.sbuf(f"{t}_qt{i}", [128, N8MAX], I32) for i in range(2)]
    sb = [p.sbuf(f"{t}_sb{i}", [128, NMAX], BF16) for i in range(2)]
    ut = [p.sbuf(f"{t}_ut{i}", [128, N8MAX], I32) for i in range(2)]
    wt = [p.sbuf(f"{t}_wt{i}", [128, NMAX], BF16) for i in range(2)]
    gi = 0
    for w in wspecs:
        K_, N, G = w["K_"], w["N"], w["G"]
        N8 = N // 8
        for g in range(G):
            b = gi % 2
            dvw = (s_dv, max(0, (gi - 1) * 16))
            q_v = p.dma("sync", qt[b][0:128, 0:N8], w["qw"][g * 128:(g + 1) * 128, :],
                        waits=[dvw], sem=s_q[b])
            sc_v = p.dma("gpsimd", sb[b][0:128, 0:N], w["sc"][g:g + 1, :].broadcast_to((128, N)),
                         waits=[dvw], sem=s_s[b])
            for s_ in range(8):
                ub = s_ % 2
                p.op("vector", lambda E, b=b, ub=ub, s_=s_, N8=N8: E.tensor_scalar(
                    ut[ub][0:128, 0:N8], qt[b][0:128, 0:N8], 4 * s_, 15,
                    ALU.logical_shift_right, ALU.bitwise_and),
                    waits=[(s_q[b], q_v), (s_s[b], sc_v),
                           (s_w[b], s_w[b].count)],
                    sem=s_dv)
                p.op("vector", lambda E, b=b, ub=ub, s_=s_, N=N, N8=N8: E.tensor_tensor(
                    wt[b][0:128, s_:N:8], ut[ub][0:128, 0:N8], sb[b][0:128, s_:N:8],
                    ALU.mult), sem=s_dv)
            p.dma("sync", w["out"][g * 128:(g + 1) * 128, :], wt[b][0:128, 0:N],
                  waits=[(s_dv, s_dv.count)], sem=s_w[b])
            gi += 1
    for wi, w in enumerate(wspecs):
        K_, N, G = w["K_"], w["N"], w["G"]
        N8 = N // 8
        p.dma("sync", qt[0][0:G, 0:N8], w["qz"], waits=[(s_dv, s_dv.count)], sem=s_z)
        p.dma("gpsimd", sb[0][0:G, 0:N], w["sc"], waits=[(s_dv, s_dv.count)], sem=s_zg)
        p.op("vector", lambda E, G=G, N=N: E.tensor_scalar_mul(
            sb[0][0:G, 0:N], sb[0][0:G, 0:N], -1.0),
            waits=[(s_z, s_z.count), (s_zg, s_zg.count),
                   (s_w[0], s_w[0].count), (s_w[1], s_w[1].count)], sem=s_dv)
        for s_ in range(8):
            p.op("vector", lambda E, G=G, N=N, N8=N8, s_=s_: E.tensor_scalar(
                ut[0][0:G, 0:N8], qt[0][0:G, 0:N8], 4 * s_, 15,
                ALU.logical_shift_right, ALU.bitwise_and), sem=s_dv)
            p.op("vector", lambda E, G=G, N=N, N8=N8, s_=s_: E.tensor_tensor(
                wt[0][0:G, s_:N:8], ut[0][0:G, 0:N8], sb[0][0:G, s_:N:8],
                ALU.mult), sem=s_dv)
        p.dma("sync", w["zs"], wt[0][0:G, 0:N], waits=[(s_dv, s_dv.count)], sem=s_z)
    p.wait("sync", (s_w[0], s_w[0].count), (s_w[1], s_w[1].count), (s_z, s_z.count))


def emit_ffn1(p: Prog, x_ap, dq, y_ap, rowbuf, xg_dram, hg_dram, tag):
    t = tag
    NI = 11008
    GI = 86
    st = emit_ln_stats(p, x_ap, rowbuf, t, want_var=True, hi_lo=False, nbuf=1)

    s_xl = [p.sem(f"sxl{i}") for i in range(2)]
    s_dvx = p.sem(t + "dvx"); s_act = p.sem(t + "act")
    s_wld = [p.sem(f"swl{i}") for i in range(2)]
    s_mm = p.sem(t + "mm"); s_sl = p.sem(t + "sl")
    s_h = p.sem(t + "h")
    s_xgs = [p.sem(t + f"rs{i}") for i in range(2)]
    s_xgl = p.sem(t + "xgl"); s_xglh = p.sem(t + "xglh")
    s_hgs = [p.sem(t + f"rs{i}") for i in range(2)]
    s_hgl = p.sem(t + "xgl"); s_hglh = p.sem(t + "xglh")
    s_zt = [p.sem(t + f"zt{i}") for i in range(4)]
    s_ev = p.sem(t + "ev")
    s_st = [p.sem(f"yst{i}") for i in range(2)]
    s_xr = [p.sem(f"sxl{i}") for i in range(2)]
    s_re = p.sem(t + "re")

    act = p.at(f"{t}_act", [128, 32 * 512], BF16, SBUF_TOP - 90112 - 32768)
    h_sb = p.at(f"{t}_h", [128, GI * 512], BF16, SBUF_TOP - 90112)
    wb = [p.sbuf(f"{t}_wb{i}", [128, 16 * 512], BF16) for i in range(2)]
    xph = [p.shared_sbuf(f"ev{i}", [128, 512], F32) for i in range(2)]
    tsubh = [p.sbuf(f"{t}_tsh{i}", [128, 512], F32) for i in range(2)]
    xg_f = tsubh[0][0:32, :]
    hg_f = tsubh[1][0:86, :]
    zst = [p.sbuf(f"{t}_zs{i}", [128, 128], BF16) for i in range(4)]
    xg_sb = p.sbuf(f"{t}_xg", [32, 1024], BF16)
    hg_sb = p.sbuf(f"{t}_hg", [GI, 1024], BF16)
    sg = [p.sbuf(f"{t}_sg{i}", [128, 512], BF16) for i in range(2)]
    yt = [p.shared_sbuf(f"yt{i}", [128, 512], F32) for i in range(2)]
    xres = [p.shared_sbuf(f"xr{i}", [128, 512], F32) for i in range(2)]
    rowe = [p.sbuf(f"{t}_re{i}", [1, 512], F32) for i in range(2)]
    ones128 = p.shared_sbuf("ones128", [128, 1], BF16)
    pp = [p.shared_psum(f"pp{i}", [128, 512]) for i in range(2)]
    po = [p.shared_psum(f"po{i}", [128, 512]) for i in range(2)]
    prowA = p.shared_psum("prowA", [1, TPC])
    ps_xg = prowA[0:1, 0:512]
    ps_hg = prowA[0:1, 512:1024]
    mean_b = st["mean_b"]
    rstd_b = st["rstd_b"]

    mm_end_tb = s_mm.count
    for tb in range(2):
        tsl = slice(tb * 512, (tb + 1) * 512)
        act_base = s_act.count
        dvx_base = s_dvx.count
        ev_tb0 = s_ev.count
        for i in range(32):
            b = i % 2
            xl_v = p.dma("scalar", xph[b][:], x_ap[i * 128:(i + 1) * 128, tsl],
                         waits=[(s_dvx, dvx_base + i - 1), (st["s_dv"], st["s_dv"].count),
                                (s_ev, ev_tb0)],
                         sem=s_xl[b])
            p.op("vector", lambda E, b=b, tsl=tsl: E.tensor_tensor(
                tsubh[b][:], xph[b][:], mean_b[:, tsl], ALU.subtract),
                waits=[(s_xl[b], xl_v), st["done"], (s_act, act_base + i - 1),
                       (s_mm, mm_end_tb)],
                sem=s_dvx)
            p.op("gpsimd", lambda E, b=b, i=i, tsl=tsl: E.tensor_tensor(
                act[:, i * 512:(i + 1) * 512], tsubh[b][:], rstd_b[:, tsl], ALU.mult),
                waits=[(s_dvx, s_dvx.count), (s_mm, mm_end_tb)], sem=s_act)
        for g in range(32):
            rb = g % 2
            p.mm(ps_xg, ones128[:], act[:, g * 512:(g + 1) * 512],
                 start=True, stop=True,
                 waits=[(s_act, act_base + g + 1), (s_re, s_re.count)],
                 sem=s_mm)
            p.op("vector", lambda E, rb=rb: E.tensor_copy(rowe[rb][:], ps_xg),
                 waits=[(s_mm, s_mm.count), (s_xgs[rb], s_xgs[rb].count),
                        (s_hgs[rb], s_hgs[rb].count)],
                 sem=s_re)
            p.dma("sync", xg_dram[g:g + 1, :], rowe[rb][:],
                  waits=[(s_re, s_re.count)], sem=s_xgs[rb])
        xgall = [(s, s.count) for s in s_xgs]
        p.dma("gpsimd", xg_sb[:, 0:512], xg_dram.ap(),
              waits=xgall + [(s_mm, mm_end_tb)], sem=s_xgl)
        p.dma("scalar", xg_f, xg_dram.ap(),
              waits=xgall, sem=s_xglh)
        p.op("vector", lambda E: E.tensor_tensor(
            xg_sb[0:32, 512:1024], xg_f, xg_sb[0:32, 0:512], ALU.subtract),
            waits=[(s_xgl, s_xgl.count), (s_xglh, s_xglh.count),
                   (s_mm, mm_end_tb)], sem=s_dvx)
        xgl = (s_dvx, s_dvx.count)
        xgl_done = 0

        h_base = s_h.count
        pair_mm = {}
        khalf_mm = {}
        for np_ in range(43):
            g0, g1 = np_ * 2, np_ * 2 + 1
            for c2 in range(2):
                g = np_ * 2 + c2
                zi = c2 * 2
                p.dma("gpsimd", zst[zi][0:32, :], dq["zsg"][0:32, g * 128:(g + 1) * 128],
                      waits=[(s_mm, pair_mm.get(np_ - 1, mm_end_tb))], sem=s_zt[zi])
                p.dma("gpsimd", zst[zi + 1][0:32, :], dq["zsu"][0:32, g * 128:(g + 1) * 128],
                      sem=s_zt[zi + 1])
            corr_w = [(s_zt[0], s_zt[0].count), (s_zt[1], s_zt[1].count),
                      (s_zt[2], s_zt[2].count), (s_zt[3], s_zt[3].count),
                      xgl, (s_act, act_base + 32),
                      (s_sl, s_sl.count), (s_h, s_h.count)]
            for c2 in range(2):
                p.mm(pp[c2][:], zst[c2 * 2][0:32, :], xg_sb[0:32, 0:512], start=True, stop=False,
                     waits=corr_w, sem=s_mm)
                p.mm(pp[c2][:], zst[c2 * 2][0:32, :], xg_sb[0:32, 512:1024], start=False, stop=False,
                     sem=s_mm)
                p.mm(po[c2][:], zst[c2 * 2 + 1][0:32, :], xg_sb[0:32, 0:512], start=True, stop=False,
                     sem=s_mm)
                p.mm(po[c2][:], zst[c2 * 2 + 1][0:32, :], xg_sb[0:32, 512:1024], start=False, stop=False,
                     sem=s_mm)
            for kh in range(2):
                b = kh
                ksl = slice(kh * 16 * 128, (kh + 1) * 16 * 128)
                p.dma("sync", wb[b][:, 0:16 * 256].rearrange("p (k c) -> p k c", c=256),
                      dq["g"][ksl, np_ * 256:(np_ + 1) * 256].rearrange(
                          "(k p) c -> p k c", p=128),
                      waits=[(s_mm, khalf_mm.get((np_ - 1, kh), mm_end_tb))], sem=s_wld[b])
                p.dma("sync", wb[b][:, 16 * 256:16 * 512].rearrange("p (k c) -> p k c", c=256),
                      dq["u"][ksl, np_ * 256:(np_ + 1) * 256].rearrange(
                          "(k p) c -> p k c", p=128),
                      sem=s_wld[b])
                wld_here = (s_wld[b], s_wld[b].count)
                for c2 in range(2):
                    for k in range(16):
                        kk = kh * 16 + k
                        p.mm(pp[c2][:], wb[b][:, k * 256 + c2 * 128: k * 256 + c2 * 128 + 128],
                             act[:, kk * 512:(kk + 1) * 512],
                             start=False, stop=(kk == 31),
                             waits=[wld_here], sem=s_mm)
                    for k in range(16):
                        kk = kh * 16 + k
                        p.mm(po[c2][:], wb[b][:, 16 * 256 + k * 256 + c2 * 128:
                                              16 * 256 + k * 256 + c2 * 128 + 128],
                             act[:, kk * 512:(kk + 1) * 512],
                             start=False, stop=(kk == 31), sem=s_mm)
                khalf_mm[(np_, kh)] = s_mm.count
            allk_mm = s_mm.count
            for c2 in range(2):
                g = np_ * 2 + c2
                p.op("scalar", lambda E, c2=c2, g=g: E.activation(
                    sg[g % 2][:], pp[c2][:], AF.Silu),
                    waits=[(s_mm, allk_mm), (s_h, max(0, s_h.count - 1))], sem=s_sl)
                p.op("vector", lambda E, c2=c2, g=g: E.scalar_tensor_tensor(
                    h_sb[:, g * 512:(g + 1) * 512], sg[g % 2][:], 0.5, po[c2][:],
                    ALU.mult, ALU.mult),
                    waits=[(s_sl, s_sl.count)], sem=s_h)
                rb = g % 2
                p.mm(ps_hg, ones128[:], h_sb[:, g * 512:(g + 1) * 512],
                     start=True, stop=True,
                     waits=[(s_h, s_h.count), (s_re, s_re.count)], sem=s_mm)
                p.op("vector", lambda E, rb=rb: E.tensor_copy(rowe[rb][:], ps_hg),
                     waits=[(s_mm, s_mm.count), (s_hgs[rb], s_hgs[rb].count),
                            (s_xgs[rb], s_xgs[rb].count)],
                     sem=s_re)
                p.dma("sync", hg_dram[g:g + 1, :], rowe[rb][:],
                      waits=[(s_re, s_re.count)], sem=s_hgs[rb])
            pair_mm[np_] = s_mm.count
        gate_end = s_mm.count
        hgall = [(s, s.count) for s in s_hgs]
        p.dma("gpsimd", hg_sb[:, 0:512], hg_dram.ap(),
              waits=hgall + [(s_mm, mm_end_tb)], sem=s_hgl)
        p.dma("scalar", hg_f, hg_dram.ap(),
              waits=hgall, sem=s_hglh)
        p.op("vector", lambda E: E.tensor_tensor(
            hg_sb[0:GI, 512:1024], hg_f, hg_sb[0:GI, 0:512], ALU.subtract),
            waits=[(s_hgl, s_hgl.count), (s_hglh, s_hglh.count),
                   (s_mm, gate_end)], sem=s_dvx)
        hgl = (s_dvx, s_dvx.count)

        dhalf_mm = {}
        for osl in range(32):
            pu = osl % 2
            zi = 2 + (osl % 2)
            war = (s_mm, max(dhalf_mm.get((osl - 1, 0), gate_end),
                             dhalf_mm.get((osl - 2, 1), 0)))
            zt_v = p.dma("gpsimd", zst[zi][0:GI, :], dq["zsd"][0:GI, osl * 128:(osl + 1) * 128],
                         waits=[war], sem=s_zt[zi])
            p.mm(po[pu][:], zst[zi][0:GI, :], hg_sb[0:GI, 0:512], start=True, stop=False,
                 waits=[(s_zt[zi], zt_v), hgl,
                        (s_ev, max(0, s_ev.count - 1)), (s_h, s_h.count)],
                 sem=s_mm)
            p.mm(po[pu][:], zst[zi][0:GI, :], hg_sb[0:GI, 512:1024], start=False, stop=False,
                 sem=s_mm)
            for ghalf in range(2):
                b2 = ghalf
                gsl = slice(ghalf * 43 * 128, (ghalf + 1) * 43 * 128)
                dw_v = p.dma("sync", wb[b2][:, 0:43 * 128].rearrange("q (g c) -> q g c", c=128),
                             dq["d"][gsl, osl * 128:(osl + 1) * 128].rearrange(
                                 "(g q) c -> q g c", q=128),
                             waits=[(s_mm, dhalf_mm.get((osl - 1, ghalf), gate_end))],
                             sem=s_wld[b2])
                for gg in range(43):
                    g = ghalf * 43 + gg
                    p.mm(po[pu][:], wb[b2][:, gg * 128:(gg + 1) * 128],
                         h_sb[:, g * 512:(g + 1) * 512],
                         start=False, stop=(g == GI - 1),
                         waits=[(s_wld[b2], dw_v), (s_h, h_base + g + 1)], sem=s_mm)
                dhalf_mm[(osl, ghalf)] = s_mm.count
            xr_v = p.dma("scalar", xres[pu][:], x_ap[osl * 128:(osl + 1) * 128, tsl],
                         waits=[(s_ev, max(0, s_ev.count - 1)),
                                (s_act, act_base + 32)], sem=s_xr[pu])
            p.op("vector", lambda E, pu=pu: E.tensor_tensor(
                yt[pu][:], po[pu][:], xres[pu][:], ALU.add),
                waits=[(s_mm, s_mm.count), (s_xr[pu], xr_v),
                       (s_st[pu], s_st[pu].count)],
                sem=s_ev)
            p.dma("sync", y_ap[osl * 128:(osl + 1) * 128, tsl], yt[pu][:],
                  waits=[(s_ev, s_ev.count)], sem=s_st[pu])
        mm_end_tb = s_mm.count
    p.wait("sync", (s_st[0], s_st[0].count), (s_st[1], s_st[1].count))
    return s_st[0]


def emit_linres(p: Prog, actsrc_ap, w_ap, x_ap, y_ap, tag, bias_ap=None,
                act_gath=None):
    t = tag
    s_a = p.sem(t + "a")
    s_wld = [p.sem(f"gwl{i}") for i in range(2)]
    s_mm = p.sem(t + "mm")
    s_ev = p.sem(t + "ev")
    s_x3 = [p.sem(f"sxl{i}") for i in range(2)]
    s_dv2 = p.sem(t + "dv2")
    s_st = [p.sem(f"yst{i}") for i in range(2)]
    s_b = p.sem(t + "b")

    act_bf = p.shared_sbuf("act_bf", [128, 32 * TPC], BF16)
    wbuf = [p.shared_sbuf(f"wb{i}", [128, 32 * 512], BF16) for i in range(2)]
    evtmp = [p.shared_sbuf(f"ev{i}", [128, 512], F32) for i in range(2)]
    xres = [p.shared_sbuf(f"xr{i}", [128, 512], F32) for i in range(2)]
    yt = [p.shared_sbuf(f"yt{i}", [128, 512], F32) for i in range(2)]
    pp = [p.shared_psum(f"pp{i}", [128, 512]) for i in range(2)]
    mm0, ev0, dv20, a0 = s_mm.count, s_ev.count, s_dv2.count, s_a.count
    if bias_ap is not None:
        bia = p.sbuf(f"{t}_bia", [128, 32], F32)
        p.dma("scalar", bia[:], bias_ap, sem=s_b)
    bias_done = (s_b, s_b.count)

    if act_gath is not None:
        gap2, idx2_dram = act_gath
        s_ix2 = p.sem("ixs")
        idx2_sb = p.sbuf(f"{t}_idx2", [128, 32], I32)
        ix2_v = p.dma("sync", idx2_sb[:], idx2_dram, sem=s_ix2)
    for i in range(32):
        if act_gath is None:
            a_all = p.dma("sync", act_bf[:, i * TPC:(i + 1) * TPC],
                          actsrc_ap[i * 128:(i + 1) * 128, :], sem=s_a)
        else:
            a_all = p.idma(act_bf[:, i * TPC:(i + 1) * TPC], gap2,
                           idx2_sb[:, i:i + 1],
                           waits=[(s_ix2, ix2_v)], sem=s_a)
    for oslab in range(8):
        b = oslab % 2
        for k in range(32):
            wl_v = p.dma("gpsimd", wbuf[b][:, k * 512:(k + 1) * 512],
                         w_ap[k * 128:(k + 1) * 128, oslab * 512:(oslab + 1) * 512],
                         waits=[(s_mm, mm0 + max(0, (oslab - 1) * 256))],
                         sem=s_wld[b])
        for osub in range(4):
            o = oslab * 4 + osub
            for tb in range(2):
                u = o * 2 + tb
                pb = u % 2
                for k in range(32):
                    p.mm(pp[pb][:], wbuf[b][:, k * 512 + osub * 128: k * 512 + osub * 128 + 128],
                         act_bf[:, k * TPC + tb * 512: k * TPC + tb * 512 + 512],
                         start=(k == 0), stop=(k == 31),
                         waits=[(s_wld[b], wl_v),
                                (s_a, a_all),
                                (s_ev, ev0 + max(0, u - 1))],
                         sem=s_mm)
                if bias_ap is not None:
                    p.op("vector", lambda E, pb=pb, o=o: E.tensor_scalar(
                        evtmp[pb][:], pp[pb][:], bia[:, o:o + 1], None, ALU.add),
                        waits=[(s_mm, s_mm.count), (s_dv2, dv20 + max(0, u - 1)), bias_done],
                        sem=s_ev)
                else:
                    p.op("vector", lambda E, pb=pb: E.tensor_copy(
                        evtmp[pb][:], pp[pb][:]),
                        waits=[(s_mm, s_mm.count), (s_dv2, dv20 + max(0, u - 1))],
                        sem=s_ev)
                xr_v = p.dma("scalar", xres[pb][:], x_ap[o * 128:(o + 1) * 128, tb * 512:(tb + 1) * 512],
                             waits=[(s_dv2, dv20 + max(0, u - 1))], sem=s_x3[pb])
                p.op("gpsimd", lambda E, pb=pb: E.tensor_tensor(
                    yt[pb][:], evtmp[pb][:], xres[pb][:], ALU.add),
                    waits=[(s_ev, s_ev.count), (s_x3[pb], xr_v),
                           (s_st[pb], s_st[pb].count)],
                    sem=s_dv2)
                p.dma("sync", y_ap[o * 128:(o + 1) * 128, tb * 512:(tb + 1) * 512],
                      yt[pb][:], waits=[(s_dv2, s_dv2.count)], sem=s_st[pb])
    p.wait("sync", (s_st[0], s_st[0].count), (s_st[1], s_st[1].count))
    return s_st[0]



def shard_tokens_fm(x):
    flat = np.ascontiguousarray(x.reshape(8192, D))
    return [np.ascontiguousarray(flat[c * TPC:(c + 1) * TPC].T) for c in range(NCORES)]


def unshard_tokens_fm(shards):
    flat = np.concatenate([s.T for s in shards], axis=0)
    return np.ascontiguousarray(flat.reshape(2, S, D))


def chunkmajor(v):
    return np.ascontiguousarray(v.reshape(32, 128).T)


def _run(nc, in_maps):
    res = run_bass_kernel_spmd(nc, in_maps, core_ids=list(range(NCORES)))
    return res.results


def host_masks():
    ki = np.arange(128)[:, None]
    qi = np.arange(512)[None, :]
    m = np.zeros((128, 2048), dtype=np.float64)
    for c in range(4):
        m[:, c * 512:(c + 1) * 512] = np.where(128 * c + ki <= qi, 0.0, -1e9)
    return np.ascontiguousarray(m.astype(NP_BF16))


def rot_tables(core, scale):
    pos = (core % 4) * 1024 + np.arange(TPC, dtype=np.float64)
    invf = 1.0 / 10000.0 ** (np.arange(0, HD, 2, dtype=np.float64) / HD)
    d = np.arange(128)
    ang = invf[d % 64][:, None] * pos[None, :]
    cos = np.cos(ang) * scale
    sin = np.sin(ang) * scale
    sgn = np.where(d < 64, -1.0, 1.0)[:, None]
    return (np.ascontiguousarray(cos.astype(NP_BF16)),
            np.ascontiguousarray((sin * sgn).astype(NP_BF16)))


def alibi_slopes(n_heads=32):
    cp2 = 2 ** math.floor(math.log2(n_heads))
    base = 2 ** (-(2 ** (-(math.log2(cp2) - 3))))
    slopes = base ** np.arange(1, cp2 + 1, dtype=np.float64)
    if cp2 != n_heads:
        extra_base = 2 ** (-(2 ** (-(math.log2(2 * cp2) - 3))))
        n_rem = min(cp2, n_heads - cp2)
        extra = extra_base ** np.arange(1, 1 + 2 * n_rem, 2, dtype=np.float64)
        slopes = np.concatenate([slopes, extra])
    return slopes


def alibi_tables(core):
    sl = alibi_slopes()
    h0 = (core % 4) * 8
    ki = np.arange(128, dtype=np.float64)
    cols = []
    for hp in range(8):
        s = sl[h0 + hp]
        for qb in range(8):
            for kt in range(4 * (qb + 1)):
                r1 = s * (ki + kt * 128 - qb * 512)
                hi = r1.astype(np.float32).astype(NP_BF16).astype(np.float64)
                lo = (r1 - hi).astype(np.float32).astype(NP_BF16)
                cols.append(np.stack([np.ones(128, np.float32).astype(NP_BF16),
                                      hi.astype(NP_BF16), lo]))
    alT = np.ascontiguousarray(np.concatenate(cols, axis=1).astype(NP_BF16))
    qi = np.arange(512, dtype=np.float64)
    qsl = np.concatenate([(-sl[h0 + hp] * qi) for hp in range(8)])
    qsl = np.ascontiguousarray(qsl.astype(np.float32).reshape(1, 4096).astype(NP_BF16))
    ones2 = np.ones((2, 4096), dtype=np.float32).astype(NP_BF16)
    return alT, qsl, ones2


def _build_ffn0():
    p = Prog("ffn0")
    x_ap = p.dram("x", [D, TPC], F32, kind="ExternalInput").ap()
    w0t_ap = p.dram("w0t", [D, D], F32, kind="ExternalInput").ap()
    alpha_ap = p.dram("alpha", [128, 32], F32, kind="ExternalInput").ap()
    bias_ap = p.dram("bias", [128, 32], F32, kind="ExternalInput").ap()
    y_ap = p.dram("y", [D, TPC], F32, kind="ExternalOutput").ap()
    sw0 = p.dram("sw0", [D, D], BF16)
    rowbuf = p.dram("rowbuf", [2, TPC], F32)
    nc = p.build(lambda p: emit_ffn0(p, x_ap, w0t_ap, alpha_ap, bias_ap, y_ap,
                                     sw0.ap(), rowbuf, "f0"))
    return nc


def _build_qkv(layer):
    p = Prog(f"qkv{layer}")
    x_ap = p.dram("x", [D, TPC], F32, kind="ExternalInput").ap()
    q_ap = p.dram("q", [D, TPC], BF16, kind="ExternalOutput").ap()
    k_ap = p.dram("k", [D, TPC], BF16, kind="ExternalOutput").ap()
    vt_ap = p.dram("vt", [TPC, D], BF16, kind="ExternalOutput").ap()
    n2_ap = p.dram("n2", [64, TPC], BF16, kind="ExternalOutput").ap()
    vfm = p.dram("vfm", [D, TPC], BF16)
    rowbuf = p.dram("rowbuf", [2, TPC], F32)
    y_outs = dict(q=q_ap, k=k_ap, v=vfm.ap(), vt=vt_ap, n2=n2_ap)
    if layer == 0:
        wq = p.dram("wq", [D, D], F32, kind="ExternalInput").ap()
        wk = p.dram("wk", [D, D], F32, kind="ExternalInput").ap()
        wv = p.dram("wv", [D, D], F32, kind="ExternalInput").ap()
        tabs = [p.dram(n, [128, TPC], BF16, kind="ExternalInput").ap()
                for n in ("cq", "sq", "ck", "sk")]
        weights = [(wq, [("q", h) for h in range(32)]),
                   (wk, [("k", h) for h in range(32)]),
                   (wv, [("v", h) for h in range(32)])]
        nc = p.build(lambda p: emit_qkv(p, x_ap, weights, y_outs, rowbuf, "qk", 0,
                                        rot_tabs=tabs))
    else:
        wa = p.dram("wa", [D, 3 * D], F32, kind="ExternalInput").ap()
        ba = p.dram("ba", [128, 96], F32, kind="ExternalInput").ap()
        roles = ([("q", h) for h in range(32)] + [("k", h) for h in range(32)]
                 + [("v", h) for h in range(32)])
        weights = [(wa, roles)]
        nc = p.build(lambda p: emit_qkv(p, x_ap, weights, y_outs, rowbuf, "qk", 1,
                                        battn_ap=ba))
    return nc


def _build_attn(layer):
    p = Prog(f"attn{layer}")
    qh_ap = p.dram("qh", [1024, S], BF16, kind="ExternalInput").ap()
    kh_ap = p.dram("kh", [1024, S], BF16, kind="ExternalInput").ap()
    vh_ap = p.dram("vh", [S, 1024], BF16, kind="ExternalInput").ap()
    n2_ap = p.dram("n2", [16, S], BF16, kind="ExternalInput").ap()
    mk_ap = p.dram("mk", [128, 2048], BF16, kind="ExternalInput").ap()
    o_ap = p.dram("o", [1024, S], BF16, kind="ExternalOutput").ap()
    browd = p.dram("browd", [1, 16], F32)
    zrowd = p.dram("zrowd", [2, 512], F32)
    if layer == 0:
        nc = p.build(lambda p: emit_attn(p, qh_ap, kh_ap, vh_ap, n2_ap, mk_ap,
                                         o_ap, browd, zrowd, "at", 0))
    else:
        alt_ap = p.dram("alt", [3, 1152 * 128], BF16, kind="ExternalInput").ap()
        qsl_ap = p.dram("qsl", [1, 4096], BF16, kind="ExternalInput").ap()
        on2_ap = p.dram("on2", [2, 4096], BF16, kind="ExternalInput").ap()
        nc = p.build(lambda p: emit_attn(p, qh_ap, kh_ap, vh_ap, n2_ap, mk_ap,
                                         o_ap, browd, zrowd, "at", 1,
                                         alt_ap=alt_ap, qsl_ap=qsl_ap,
                                         ones2_ap=on2_ap))
    return nc


def _build_proj():
    p = Prog("proj")
    a_ap = p.dram("a", [D, TPC], BF16, kind="ExternalInput").ap()
    w_ap = p.dram("w", [D, D], F32, kind="ExternalInput").ap()
    x_ap = p.dram("x", [D, TPC], F32, kind="ExternalInput").ap()
    b_ap = p.dram("b", [128, 32], F32, kind="ExternalInput").ap()
    y_ap = p.dram("y", [D, TPC], F32, kind="ExternalOutput").ap()
    nc = p.build(lambda p: emit_linres(p, a_ap, w_ap, x_ap, y_ap, "pr", bias_ap=b_ap))
    return nc


def _build_deq():
    p = Prog("deq")
    def din(n, sh, dt):
        return p.dram(n, sh, dt, kind="ExternalInput").ap()
    wspecs = [
        dict(qw=din("gq", [4096, 1376], I32), sc=din("gs", [32, 11008], F32),
             qz=din("gz", [32, 1376], I32),
             out=p.dram("dg", [4096, 11008], BF16, kind="ExternalOutput").ap(),
             zs=p.dram("zg", [32, 11008], BF16, kind="ExternalOutput").ap(),
             K_=4096, N=11008, G=32),
        dict(qw=din("uq", [4096, 1376], I32), sc=din("us", [32, 11008], F32),
             qz=din("uz", [32, 1376], I32),
             out=p.dram("du", [4096, 11008], BF16, kind="ExternalOutput").ap(),
             zs=p.dram("zu", [32, 11008], BF16, kind="ExternalOutput").ap(),
             K_=4096, N=11008, G=32),
        dict(qw=din("dq", [11008, 512], I32), sc=din("ds", [86, 4096], F32),
             qz=din("dz", [86, 512], I32),
             out=p.dram("dd", [11008, 4096], BF16, kind="ExternalOutput").ap(),
             zs=p.dram("zd", [86, 4096], BF16, kind="ExternalOutput").ap(),
             K_=11008, N=4096, G=86),
    ]
    nc = p.build(lambda p: emit_dequant(p, wspecs, "dq"))
    return nc


def _build_ffn1():
    p = Prog("ffn1c")
    x_ap = p.dram("x", [D, TPC], F32, kind="ExternalInput").ap()
    dqd = dict(
        g=p.dram("dg", [4096, 11008], BF16, kind="ExternalInput").ap(),
        u=p.dram("du", [4096, 11008], BF16, kind="ExternalInput").ap(),
        d=p.dram("dd", [11008, 4096], BF16, kind="ExternalInput").ap(),
        zsg=p.dram("zg", [32, 11008], BF16, kind="ExternalInput").ap(),
        zsu=p.dram("zu", [32, 11008], BF16, kind="ExternalInput").ap(),
        zsd=p.dram("zd", [86, 4096], BF16, kind="ExternalInput").ap(),
    )
    y_ap = p.dram("y", [D, TPC], F32, kind="ExternalOutput").ap()
    rowbuf = p.dram("rowbuf", [2, TPC], F32)
    xg_d = p.dram("xgd", [32, 512], F32)
    hg_d = p.dram("hgd", [86, 512], F32)
    nc = p.build(lambda p: emit_ffn1(p, x_ap, dqd, y_ap, rowbuf, xg_d, hg_d, "f1"))
    return nc


def _to_headshard(res, key_q="q", key_k="k", key_vt="vt", key_n2="n2"):
    out = []
    for c in range(NCORES):
        b, h0 = c // 4, (c % 4) * 8
        js = range(b * 4, b * 4 + 4)
        qf = np.concatenate([res[j][key_q] for j in js], axis=1)
        kf = np.concatenate([res[j][key_k] for j in js], axis=1)
        vtf = np.concatenate([res[j][key_vt] for j in js], axis=0)
        n2f = np.concatenate([res[j][key_n2] for j in js], axis=1)
        out.append(dict(
            qh=np.ascontiguousarray(qf[h0 * 128:(h0 + 8) * 128]),
            kh=np.ascontiguousarray(kf[h0 * 128:(h0 + 8) * 128]),
            vh=np.ascontiguousarray(vtf[:, h0 * 128:(h0 + 8) * 128]),
            n2=np.ascontiguousarray(np.concatenate(
                [n2f[h0:h0 + 8], n2f[32 + h0:32 + h0 + 8]], axis=0)),
        ))
    return out


def _from_headshard(res, key="o"):
    out = []
    for c in range(NCORES):
        b, j = c // 4, c % 4
        full = np.concatenate([res[cc][key] for cc in range(b * 4, b * 4 + 4)], axis=0)
        out.append(np.ascontiguousarray(full[:, j * 1024:(j + 1) * 1024]))
    return out


RB1 = 12352
RB2 = 4096
RG = [[0, 1, 2, 3], [4, 5, 6, 7]]


def host_idx(core):
    c4 = core % 4
    h0 = c4 * 8
    ar = np.arange(128, dtype=np.int64)
    cols = []
    for l in range(2):
        for h in range(8):
            for r in range(4):
                cols.append(r * RB1 + (h0 + h) * 128 + ar)
        for kt in range(32):
            r, tl = kt // 8, kt % 8
            cols.append(r * RB1 + 8192 + (tl * 128 + ar) * 4 + c4)
        for r in range(4):
            v = np.zeros(128, np.int64)
            idx16 = np.concatenate([h0 + np.arange(8), 32 + h0 + np.arange(8)])
            v[:16] = r * RB1 + 12288 + idx16
            cols.append(v)
        for h in range(8):
            for qb in range(8):
                r = qb // 2
                cols.append(r * RB1 + 4096 + (h0 + h) * 128 + ar)
        for i in range(32):
            r = i // 8
            cols.append(r * RB2 + ((i % 8) * 128 + ar) * 4 + c4)
    return np.ascontiguousarray(np.stack(cols, axis=1).astype(np.int32))


def _build_fused():
    p = Prog("fused")
    x_in = p.dram("x", [D, TPC], F32, kind="ExternalInput").ap()
    y_out = p.dram("y", [D, TPC], F32, kind="ExternalOutput").ap()
    w0t_ap = p.dram("w0t", [D, D], F32, kind="ExternalInput").ap()
    alpha_ap = p.dram("alpha", [128, 32], F32, kind="ExternalInput").ap()
    bias_ap = p.dram("bias", [128, 32], F32, kind="ExternalInput").ap()
    wq = p.dram("wq", [D, D], F32, kind="ExternalInput").ap()
    wk = p.dram("wk", [D, D], F32, kind="ExternalInput").ap()
    wv = p.dram("wv", [D, D], F32, kind="ExternalInput").ap()
    wo = p.dram("wo", [D, D], F32, kind="ExternalInput").ap()
    wa = p.dram("wa", [D, 3 * D], F32, kind="ExternalInput").ap()
    ba = p.dram("ba", [128, 96], F32, kind="ExternalInput").ap()
    wp = p.dram("wp", [D, D], F32, kind="ExternalInput").ap()
    bp = p.dram("bp", [128, 32], F32, kind="ExternalInput").ap()
    tabs = [p.dram(n, [128, TPC], BF16, kind="ExternalInput").ap()
            for n in ("cq", "sq", "ck", "sk")]
    mk_ap = p.dram("mk", [128, 2048], BF16, kind="ExternalInput").ap()
    alt_ap = p.dram("alt", [3, 1152 * 128], BF16, kind="ExternalInput").ap()
    qsl_ap = p.dram("qsl", [1, 4096], BF16, kind="ExternalInput").ap()
    on2_ap = p.dram("on2", [2, 4096], BF16, kind="ExternalInput").ap()
    idx_ap = p.dram("idx", [128, 328], I32, kind="ExternalInput").ap()
    qin = {}
    for n, sh, dt in [("gq", [4096, 1376], I32), ("gs", [32, 11008], F32),
                      ("gz", [32, 1376], I32), ("uq", [4096, 1376], I32),
                      ("us", [32, 11008], F32), ("uz", [32, 1376], I32),
                      ("dq", [11008, 512], I32), ("ds", [86, 4096], F32),
                      ("dz", [86, 512], I32)]:
        qin[n] = p.dram(n, sh, dt, kind="ExternalInput").ap()

    XA = p.dram("XA", [D, TPC], F32)
    XB = p.dram("XB", [D, TPC], F32)
    sw0 = p.dram("sw0", [D, D], BF16)
    rowbuf = p.dram("rowbuf", [2, TPC], F32)
    xg_d = p.dram("xgd", [32, 512], F32)
    hg_d = p.dram("hgd", [86, 512], F32)
    browd = p.dram("browd", [1, 16], F32)
    zrowd = p.dram("zrowd", [2, 512], F32)
    dg = p.dram("dg", [4096, 11008], BF16)
    du = p.dram("du", [4096, 11008], BF16)
    dd = p.dram("dd", [11008, 4096], BF16)
    zg = p.dram("zg", [32, 11008], BF16)
    zu = p.dram("zu", [32, 11008], BF16)
    zd = p.dram("zd", [86, 4096], BF16)
    g1in = p.dram("g1in", [RB1, 1024], BF16)
    g1out = p.dram("g1out", [4 * RB1, 1024], BF16)
    g2in = p.dram("g2in", [RB2, 1024], BF16)
    g2out = p.dram("g2out", [4 * RB2, 1024], BF16)

    wspecs = [
        dict(qw=qin["gq"], sc=qin["gs"], qz=qin["gz"], out=dg.ap(), zs=zg.ap(),
             K_=4096, N=11008, G=32),
        dict(qw=qin["uq"], sc=qin["us"], qz=qin["uz"], out=du.ap(), zs=zu.ap(),
             K_=4096, N=11008, G=32),
        dict(qw=qin["dq"], sc=qin["ds"], qz=qin["dz"], out=dd.ap(), zs=zd.ap(),
             K_=11008, N=4096, G=86),
    ]
    dqd = dict(g=dg.ap(), u=du.ap(), d=dd.ap(), zsg=zg.ap(), zsu=zu.ap(), zsd=zd.ap())
    y_outs = dict(
        q=g1in.ap()[4096:8192, :],
        k=g1in.ap()[0:4096, :],
        v=p.dram("vfm", [D, TPC], BF16).ap(),
        vt=g1in.ap()[8192:12288, :].rearrange("(t r) c -> t (r c)", r=4),
        n2=g1in.ap()[12288:12352, :],
    )

    def emit(p):
        with p.scope():
            emit_dequant(p, wspecs, "dq")
        p.barrier()
        with p.scope():
            emit_ffn0(p, x_in, w0t_ap, alpha_ap, bias_ap, XA.ap(), sw0.ap(),
                      rowbuf, "f0")
        p.barrier()
        for layer in range(2):
            xcur = XA if layer == 0 else XB
            xnext = XB if layer == 0 else XA
            with p.scope():
                if layer == 0:
                    weights = [(wq, [("q", h) for h in range(32)]),
                               (wk, [("k", h) for h in range(32)]),
                               (wv, [("v", h) for h in range(32)])]
                    emit_qkv(p, xcur.ap(), weights, y_outs, rowbuf, "qk", 0,
                             rot_tabs=tabs)
                else:
                    roles = ([("q", h) for h in range(32)] + [("k", h) for h in range(32)]
                             + [("v", h) for h in range(32)])
                    emit_qkv(p, xcur.ap(), [(wa, roles)], y_outs, rowbuf, "qk", 1,
                             battn_ap=ba)
            p.barrier()
            s_cc = p.sem("cc1")
            p.op("gpsimd", lambda E: E.collective_compute(
                "AllGather", ALU.bypass, replica_groups=RG,
                ins=[g1in.ap()], outs=[g1out.ap()]), sem=s_cc)
            p.barrier()
            with p.scope():
                ib = layer * 164
                o_view = g2in.ap()[0:4096, :].rearrange("(hd r) c -> hd (r c)", r=4)
                if layer == 0:
                    emit_attn(p, None, None, None, None, mk_ap, o_view, browd,
                              zrowd, "at", 0,
                              gath=(g1out.ap(), idx_ap[:, ib:ib + 132], 0))
                else:
                    emit_attn(p, None, None, None, None, mk_ap, o_view, browd,
                              zrowd, "at", 1, alt_ap=alt_ap, qsl_ap=qsl_ap,
                              ones2_ap=on2_ap,
                              gath=(g1out.ap(), idx_ap[:, ib:ib + 132], 0))
            p.barrier()
            s_cc2 = p.sem("cc2")
            p.op("gpsimd", lambda E: E.collective_compute(
                "AllGather", ALU.bypass, replica_groups=RG,
                ins=[g2in.ap()], outs=[g2out.ap()]), sem=s_cc2)
            p.barrier()
            with p.scope():
                ib = layer * 164 + 132
                emit_linres(p, None, wo if layer == 0 else wp, xcur.ap(),
                            xnext.ap(), "pr",
                            bias_ap=(None if layer == 0 else bp),
                            act_gath=(g2out.ap(), idx_ap[:, ib:ib + 32]))
            p.barrier()
            with p.scope():
                if layer == 0:
                    emit_ffn0(p, xnext.ap(), w0t_ap, alpha_ap, bias_ap, XA.ap(),
                              sw0.ap(), rowbuf, "f0")
                else:
                    emit_ffn1(p, xnext.ap(), dqd, y_out, rowbuf, xg_d, hg_d, "f1")
            p.barrier()
            if layer == 0:
                with p.scope():
                    emit_ffn1(p, XA.ap(), dqd, XB.ap(), rowbuf, xg_d, hg_d, "f1")
                p.barrier()
    nc = p.build(emit)
    return nc


_PROGS = {}


def _prog(name, builder):
    if name not in _PROGS:
        _PROGS[name] = builder()
    return _PROGS[name]


FUSED = True


def kernel(**inputs):
    if FUSED:
        return kernel_fused(**inputs)
    return kernel_unfused(**inputs)


def kernel_fused(**inputs):
    f32 = np.float32
    x = np.asarray(inputs["L0_attn_hidden_states"], dtype=f32)
    shards = shard_tokens_fm(x)
    w0t = np.ascontiguousarray(np.asarray(inputs["L0_ffn_weight"], f32).T)
    alpha_cm = chunkmajor(np.asarray(inputs["L0_ffn_alpha"], f32).reshape(-1))
    bias_cm = chunkmajor(np.asarray(inputs["L0_ffn_bias"], f32))
    ba_cm = np.ascontiguousarray(
        np.asarray(inputs["L1_attn_b_attn"], f32).reshape(96, 128).T)
    bp_cm = chunkmajor(np.asarray(inputs["L1_attn_b_proj"], f32))
    mk = host_masks()
    sc0 = 1.0 / math.sqrt(HD)
    nc = _prog("fused", _build_fused)
    im = []
    for c in range(NCORES):
        cq, sq_ = rot_tables(c, sc0)
        ck, sk_ = rot_tables(c, 1.0)
        alT, qsl, on2 = alibi_tables(c)
        im.append(dict(
            x=shards[c], w0t=w0t, alpha=alpha_cm, bias=bias_cm,
            wq=np.asarray(inputs["L0_attn_wq"], f32),
            wk=np.asarray(inputs["L0_attn_wk"], f32),
            wv=np.asarray(inputs["L0_attn_wv"], f32),
            wo=np.asarray(inputs["L0_attn_wo"], f32),
            wa=np.asarray(inputs["L1_attn_w_attn"], f32),
            ba=ba_cm,
            wp=np.asarray(inputs["L1_attn_w_proj"], f32),
            bp=bp_cm,
            cq=cq, sq=sq_, ck=ck, sk=sk_, mk=mk, alt=alT, qsl=qsl, on2=on2,
            idx=host_idx(c),
            gq=inputs["L1_ffn_gate_qweight"],
            gs=np.asarray(inputs["L1_ffn_gate_scales"], f32),
            gz=inputs["L1_ffn_gate_qzeros"],
            uq=inputs["L1_ffn_up_qweight"],
            us=np.asarray(inputs["L1_ffn_up_scales"], f32),
            uz=inputs["L1_ffn_up_qzeros"],
            dq=inputs["L1_ffn_down_qweight"],
            ds=np.asarray(inputs["L1_ffn_down_scales"], f32),
            dz=inputs["L1_ffn_down_qzeros"],
        ))
    res = _run(nc, im)
    return unshard_tokens_fm([res[c]["y"] for c in range(NCORES)]).astype(np.float32)


def kernel_unfused(**inputs):
    f32 = np.float32
    x = np.asarray(inputs["L0_attn_hidden_states"], dtype=f32)
    w0t = np.ascontiguousarray(np.asarray(inputs["L0_ffn_weight"], f32).T)
    alpha_cm = chunkmajor(np.asarray(inputs["L0_ffn_alpha"], f32).reshape(-1))
    bias_cm = chunkmajor(np.asarray(inputs["L0_ffn_bias"], f32))
    wq = np.asarray(inputs["L0_attn_wq"], f32)
    wk = np.asarray(inputs["L0_attn_wk"], f32)
    wv = np.asarray(inputs["L0_attn_wv"], f32)
    wo = np.asarray(inputs["L0_attn_wo"], f32)
    wa = np.asarray(inputs["L1_attn_w_attn"], f32)
    ba_cm = np.ascontiguousarray(
        np.asarray(inputs["L1_attn_b_attn"], f32).reshape(96, 128).T)
    wp = np.asarray(inputs["L1_attn_w_proj"], f32)
    bp_cm = chunkmajor(np.asarray(inputs["L1_attn_b_proj"], f32))
    zeros_cm = np.zeros((128, 32), f32)
    mk = host_masks()
    sc0 = 1.0 / math.sqrt(HD)

    x0 = shard_tokens_fm(x)

    nc = _prog("deq", _build_deq)
    im = [dict(gq=inputs["L1_ffn_gate_qweight"], gs=np.asarray(inputs["L1_ffn_gate_scales"], f32),
               gz=inputs["L1_ffn_gate_qzeros"],
               uq=inputs["L1_ffn_up_qweight"], us=np.asarray(inputs["L1_ffn_up_scales"], f32),
               uz=inputs["L1_ffn_up_qzeros"],
               dq=inputs["L1_ffn_down_qweight"], ds=np.asarray(inputs["L1_ffn_down_scales"], f32),
               dz=inputs["L1_ffn_down_qzeros"])] * NCORES
    rdq = _run(nc, im)

    def ffn0(xs):
        nc = _prog("ffn0", _build_ffn0)
        im = [dict(x=xs[c], w0t=w0t, alpha=alpha_cm, bias=bias_cm) for c in range(NCORES)]
        r = _run(nc, im)
        return [r[c]["y"] for c in range(NCORES)]

    def ffn1(xs):
        nc = _prog("ffn1", _build_ffn1)
        im = [dict(x=xs[c], dg=rdq[c]["dg"], du=rdq[c]["du"], dd=rdq[c]["dd"],
                   zg=rdq[c]["zg"], zu=rdq[c]["zu"], zd=rdq[c]["zd"])
              for c in range(NCORES)]
        r = _run(nc, im)
        return [r[c]["y"] for c in range(NCORES)]

    def attn_block(xs, layer):
        if layer == 0:
            nc = _prog("qkv0", lambda: _build_qkv(0))
            im = []
            for c in range(NCORES):
                cq, sq_ = rot_tables(c, sc0)
                ck, sk_ = rot_tables(c, 1.0)
                im.append(dict(x=xs[c], wq=wq, wk=wk, wv=wv, cq=cq, sq=sq_, ck=ck, sk=sk_))
        else:
            nc = _prog("qkv1", lambda: _build_qkv(1))
            im = [dict(x=xs[c], wa=wa, ba=ba_cm) for c in range(NCORES)]
        r = _run(nc, im)
        hs = _to_headshard(r)
        if layer == 0:
            nc = _prog("attn0", lambda: _build_attn(0))
            im = [dict(mk=mk, **hs[c]) for c in range(NCORES)]
        else:
            nc = _prog("attn1", lambda: _build_attn(1))
            im = []
            for c in range(NCORES):
                alT, qsl, on2 = alibi_tables(c)
                im.append(dict(mk=mk, alt=alT, qsl=qsl, on2=on2, **hs[c]))
        r = _run(nc, im)
        afm = _from_headshard(r)
        nc = _prog("proj", _build_proj)
        wmat = wo if layer == 0 else wp
        bmat = zeros_cm if layer == 0 else bp_cm
        im = [dict(a=afm[c], w=wmat, x=xs[c], b=bmat) for c in range(NCORES)]
        r = _run(nc, im)
        return [r[c]["y"] for c in range(NCORES)]

    x1 = ffn0(x0)
    x2 = attn_block(x1, 0)
    x3 = ffn0(x2)
    x4 = ffn1(x3)
    x5 = attn_block(x4, 1)
    x6 = ffn1(x5)
    global LAST_STAGES
    LAST_STAGES = [x1, x2, x3, x4, x5, x6]
    return unshard_tokens_fm(x6).astype(np.float32)


LAST_STAGES = None


# version 04 (final)
# speedup vs baseline: 1.0308x; 1.0308x over previous
import math
from contextlib import ExitStack

import numpy as np
import ml_dtypes

import concourse.bass as bass
import concourse.mybir as mybir
from concourse.bass import IndirectOffsetOnAxis
from concourse.bass_utils import run_bass_kernel_spmd

F32 = mybir.dt.float32
BF16 = mybir.dt.bfloat16
I32 = mybir.dt.int32
AF = mybir.ActivationFunctionType
ALU = mybir.AluOpType
NP_BF16 = ml_dtypes.bfloat16

NCORES = 8
D = 4096
TPC = 1024
H = 32
HD = 128
S = 4096
HPC = 8
EPS = 1e-5

ENGINES = ("sync", "tensor", "vector", "scalar", "gpsimd")
SBUF_TOP = 229344


class Sem:
    __slots__ = ("h", "count", "name")

    def __init__(self, h, name):
        self.h = h
        self.name = name
        self.count = 0


class Prog:
    def __init__(self, name="prog"):
        self.nc = bass.Bass("TRN2", target_bir_lowering=False)
        self.name = name
        self.ops = {e: [] for e in ENGINES}
        self._shared = {}
        self._es = None
        self._last_wait = {e: {} for e in ENGINES}
        self._arena_low = SBUF_TOP
        self._auto = {}
        self._uid = 0

    def uid(self):
        self._uid += 1
        return self._uid

    def dram(self, name, shape, dtype, kind="Internal"):
        return self.nc.dram_tensor(name, list(shape), dtype, kind=kind)

    def sbuf(self, name, shape, dtype):
        assert self._es is not None
        return self._es.enter_context(
            self.nc.sbuf_tensor(f"{name}_{self.uid()}", list(shape), dtype))

    def psum(self, name, shape, dtype=F32):
        assert self._es is not None
        return self._root_es.enter_context(
            self.nc.psum_tensor(f"{name}_{self.uid()}", list(shape), dtype))

    def sem(self, name):
        assert self._es is not None
        if not hasattr(self, "_sems"):
            self._sems = []
            self._sem_cache = {}
        if name in self._sem_cache:
            return self._sem_cache[name]
        h = self._root_es.enter_context(self.nc.semaphore(f"{name}_{self.uid()}"))
        s = Sem(h, name)
        self._sems.append(s)
        self._sem_cache[name] = s
        return s

    def scope(self):
        from contextlib import contextmanager

        @contextmanager
        def _cm():
            outer_es = self._es
            outer_shared = self._shared
            self._shared = {}
            with ExitStack() as es:
                self._es = es
                try:
                    yield
                finally:
                    self._es = outer_es
                    self._shared = outer_shared
        return _cm()

    def barrier(self):
        sv = [(s, s.count) for s in getattr(self, "_sems", []) if s.count > 0]
        for e in ENGINES:
            self.wait(e, *sv)

    def shared_sbuf(self, name, shape, dtype):
        key = ("sb", name)
        if key in self._shared:
            h, sh, dt = self._shared[key]
            assert list(sh) == list(shape) and dt == dtype, f"shared sbuf {name} mismatch"
            return h
        h = self.sbuf(name, shape, dtype)
        self._shared[key] = (h, list(shape), dtype)
        return h

    def at(self, name, shape, dtype, offset):
        self._arena_low = min(self._arena_low, offset)
        return self.nc.alloc_sbuf_tensor_at(f"{name}_{self.uid()}", list(shape), dtype,
                                            offset=offset)

    def shared_psum(self, name, shape, dtype=F32):
        if not hasattr(self, "_shared_ps"):
            self._shared_ps = {}
        if name in self._shared_ps:
            h, sh, dt = self._shared_ps[name]
            assert list(sh) == list(shape) and dt == dtype, f"shared psum {name} mismatch"
            return h
        h = self.psum(name, shape, dtype)
        self._shared_ps[name] = (h, list(shape), dtype)
        return h

    def _emit(self, eng, waits, fn, sem, inc, is_dma=False):
        lw = self._last_wait[eng]
        if not is_dma and eng in ("vector", "scalar", "gpsimd"):
            if sem is None:
                if eng not in self._auto:
                    self._auto[eng] = self.sem("auto_" + eng)
                sem = self._auto[eng]
                inc = 1
            prev = getattr(self, "_prev_op", {}).get(eng)
            if prev is not None:
                waits = list(waits) + [prev]
        filt = []
        for s, v in waits:
            if v <= 0:
                continue
            if lw.get(id(s), -1) >= v:
                continue
            lw[id(s)] = v
            filt.append((s.h, v))
        if sem is not None:
            sem.count += inc
            if not is_dma and eng in ("vector", "scalar", "gpsimd"):
                if not hasattr(self, "_prev_op"):
                    self._prev_op = {}
                self._prev_op[eng] = (sem, sem.count)

        def closure(E, fn=fn, filt=tuple(filt), semh=(sem.h if sem else None), inc=inc):
            for h, v in filt:
                E.wait_ge(h, v)
            inst = fn(E)
            if semh is not None:
                inst.then_inc(semh, inc)

        self.ops[eng].append(closure)
        return sem.count if sem else None

    def op(self, eng, fn, waits=(), sem=None, inc=1):
        return self._emit(eng, waits, fn, sem, inc)

    def dma(self, eng, out, in_, waits=(), sem=None, **kw):
        return self._emit(eng, waits, lambda E: E.dma_start(out=out, in_=in_, **kw),
                          sem, 16, is_dma=True)

    def idma(self, out, in_, idx_ap, waits=(), sem=None, eoff=0):
        return self._emit("gpsimd", waits,
                          lambda E: E.indirect_dma_start(
                              out, None, in_, IndirectOffsetOnAxis(idx_ap, 0),
                              element_offset=eoff),
                          sem, 16, is_dma=True)

    def dma_tr(self, eng, out, in_, waits=(), sem=None):
        return self._emit(eng, waits, lambda E: E.dma_start_transpose(out, in_),
                          sem, 16, is_dma=True)

    def wait(self, eng, *sems_vals):
        lw = self._last_wait[eng]
        filt = []
        for s, v in sems_vals:
            if v <= 0 or lw.get(id(s), -1) >= v:
                continue
            lw[id(s)] = v
            filt.append((s.h, v))
        if not filt:
            return

        def closure(E, filt=tuple(filt)):
            for h, v in filt:
                E.wait_ge(h, v)

        self.ops[eng].append(closure)

    def mm(self, out, lhsT, rhs, start, stop, waits=(), sem=None):
        return self._emit(
            "tensor", waits,
            lambda E: E.matmul(out, lhsT, rhs, start=start, stop=stop), sem, 1)

    def build(self, emit_fn):
        with ExitStack() as es:
            self._es = es
            self._root_es = es
            emit_fn(self)
            assert self.nc.sbuf_bytes_remaining >= SBUF_TOP - self._arena_low, (
                f"bump allocations overlap arena: remaining "
                f"{self.nc.sbuf_bytes_remaining} < arena {SBUF_TOP - self._arena_low}")
            with self.nc.Block() as block:
                for e in ENGINES:
                    lst = self.ops[e]
                    if not lst:
                        continue
                    getattr(block, e)(lambda E, lst=lst: [f(E) for f in lst])
        self._es = None
        return self.nc



def emit_ln_stats(p: Prog, x_ap, rowbuf, tag, want_var, hi_lo, nbuf=2):
    s_x = [p.sem(f"sxl{i}") for i in range(nbuf)]
    s_dv = p.sem("ls_dv"); s_mm = p.sem("ls_mm")
    s_sq = p.sem("ls_sq"); s_row = p.sem("ls_row")
    s_row2 = p.sem("ls_rw2"); s_g = p.sem("ls_g")

    xf32 = [p.shared_sbuf(f"xp{i}", [128, TPC], F32) for i in range(nbuf)]
    xhi = [p.shared_sbuf(f"xh{i}", [128, TPC], BF16) for i in range(nbuf)]
    xlo = [p.shared_sbuf(f"xl{i}", [128, TPC], BF16) for i in range(nbuf)] if hi_lo else None
    xsq = [p.shared_sbuf(f"xl{i}", [128, TPC], BF16) for i in range(nbuf)] if want_var else None
    ones128 = p.shared_sbuf("ones128", [128, 1], BF16)
    mean_row = p.sbuf(f"{tag}_mrow", [1, TPC], F32)
    mean_b = p.shared_sbuf("mean_b", [128, TPC], F32)
    ps_mean = p.shared_psum("prowA", [1, TPC])
    if want_var:
        rstd_row = p.sbuf(f"{tag}_rrow", [1, TPC], F32)
        sd_row = p.sbuf(f"{tag}_sdrow", [1, TPC], F32)
        rstd_b = p.shared_sbuf("rstd_b", [128, TPC], F32)
        ps_sq = p.shared_psum("prowB", [1, TPC])

    s_ms = p.sem("ls_ms")
    dv0, mm0, sq0 = s_dv.count, s_mm.count, s_sq.count
    p.op("gpsimd", lambda E: E.memset(ones128[:], 1.0), sem=s_ms)
    ms_v = s_ms.count
    n_dve = (2 if hi_lo else 1)
    mm_per = (2 if hi_lo else 1) * 2 + (2 if want_var else 0)
    for i in range(32):
        b = i % nbuf
        iw = i - nbuf + 1
        xv = p.dma("scalar", xf32[b][:], x_ap[i * 128:(i + 1) * 128, :],
                   waits=[(s_dv, dv0 + n_dve * iw)], sem=s_x[b])
        p.op("vector", lambda E, b=b: E.tensor_copy(xhi[b][:], xf32[b][:]),
             waits=[(s_x[b], xv), (s_mm, mm0 + max(0, mm_per * iw)),
                    (s_sq, (sq0 + max(0, iw)) if want_var else 0)], sem=s_dv)
        if hi_lo:
            p.op("vector", lambda E, b=b: E.tensor_tensor(
                xlo[b][:], xf32[b][:], xhi[b][:], ALU.subtract), sem=s_dv)
        if want_var:
            p.op("scalar", lambda E, b=b: E.activation(xsq[b][:], xhi[b][:], AF.Square),
                 waits=[(s_dv, dv0 + n_dve * i + 1)], sem=s_sq)
        for half in range(2):
            sl = slice(half * 512, (half + 1) * 512)
            p.mm(ps_mean[0:1, sl], ones128[:], xhi[b][:, sl],
                 start=(i == 0), stop=(not hi_lo and i == 31),
                 waits=[(s_dv, dv0 + n_dve * i + n_dve), (s_ms, ms_v),
                        (s_sq, (sq0 + i + 1) if want_var else 0)], sem=s_mm)
            if hi_lo:
                p.mm(ps_mean[0:1, sl], ones128[:], xlo[b][:, sl],
                     start=False, stop=(i == 31), sem=s_mm)
            if want_var:
                p.mm(ps_sq[0:1, sl], ones128[:], xsq[b][:, sl],
                     start=(i == 0), stop=(i == 31), sem=s_mm)
    mm_end = s_mm.count
    p.op("vector", lambda E: E.tensor_scalar_mul(mean_row[:], ps_mean[0:1, :], 1.0 / D),
         waits=[(s_mm, mm_end)], sem=s_dv)
    rv = p.dma("sync", rowbuf[0:1, :], mean_row[:], waits=[(s_dv, s_dv.count)], sem=s_row)
    p.dma("gpsimd", mean_b[:], rowbuf[0:1, :].broadcast_to((128, TPC)),
          waits=[(s_row, rv)], sem=s_g)
    if want_var:
        p.op("vector", lambda E: E.tensor_scalar_mul(sd_row[:], ps_sq[0:1, :], 1.0 / D),
             waits=[(s_mm, mm_end)], sem=s_dv)
        p.op("vector", lambda E: E.tensor_tensor(rstd_row[:], mean_row[:], mean_row[:], ALU.mult),
             sem=s_dv)
        p.op("vector", lambda E: E.tensor_tensor(sd_row[:], sd_row[:], rstd_row[:], ALU.subtract),
             sem=s_dv)
        p.op("scalar", lambda E: E.activation(sd_row[:], sd_row[:], AF.Ln, bias=np.float32(EPS)),
             waits=[(s_dv, s_dv.count)], sem=s_sq)
        p.op("scalar", lambda E: E.activation(rstd_row[:], sd_row[:], AF.Exp, scale=np.float32(-0.5)),
             sem=s_sq)
        rv2 = p.dma("sync", rowbuf[1:2, :], rstd_row[:], waits=[(s_sq, s_sq.count)], sem=s_row2)
        p.dma("gpsimd", rstd_b[:], rowbuf[1:2, :].broadcast_to((128, TPC)),
              waits=[(s_row2, rv2)], sem=s_g)
    done = (s_g, s_g.count)
    return dict(mean_b=mean_b, rstd_b=(rstd_b if want_var else None), done=done,
                s_dv=s_dv)


def emit_xln(p: Prog, x_ap, st, tag, sign=False):
    s_x2 = [p.sem(f"xlx{i}") for i in range(2)]
    s_dv = p.sem("xl_dv"); s_o = p.sem("xl_o")
    dv0, o0 = s_dv.count, s_o.count
    xf32 = [p.shared_sbuf(f"xp{i}", [128, TPC], F32) for i in range(2)]
    tsub = [p.shared_sbuf(f"tsb{i}", [128, TPC], F32) for i in range(2)]
    act_bf = p.shared_sbuf("act_bf", [128, 32 * TPC], BF16)
    mean_b = st["mean_b"]
    for i in range(32):
        b = i % 2
        xv = p.dma("scalar", xf32[b][:], x_ap[i * 128:(i + 1) * 128, :],
                   waits=[(s_dv, dv0 + i - 1), (st["s_dv"], st["s_dv"].count)], sem=s_x2[b])
        p.op("vector", lambda E, b=b: E.tensor_tensor(tsub[b][:], xf32[b][:], mean_b[:], ALU.subtract),
             waits=[(s_x2[b], xv), st["done"], (s_o, o0 + i - 1)], sem=s_dv)
        if sign:
            p.op("scalar", lambda E, b=b, i=i: E.activation(
                act_bf[:, i * TPC:(i + 1) * TPC], tsub[b][:], AF.Sign),
                waits=[(s_dv, s_dv.count)], sem=s_o)
        else:
            p.op("gpsimd", lambda E, b=b, i=i: E.tensor_tensor(
                act_bf[:, i * TPC:(i + 1) * TPC], tsub[b][:], st["rstd_b"][:], ALU.mult),
                waits=[(s_dv, s_dv.count)], sem=s_o)
    return act_bf, s_o, o0


def emit_ffn0(p: Prog, x_ap, w0t_ap, alpha_ap, bias_ap, y_ap, sw0, rowbuf, tag):
    t = tag
    s_wld = [p.sem(f"gwl{i}") for i in range(2)]
    s_sg = p.sem(t + "sg")
    s_sw = [p.sem(t + f"sw{i}") for i in range(2)]
    s_wld2 = [p.sem(f"swl{i}") for i in range(2)]
    s_mm2 = p.sem(t + "mm2"); s_ev = p.sem(t + "ev")
    s_x3 = [p.sem(f"sxl{i}") for i in range(2)]
    s_dv2 = p.sem(t + "dv2")
    s_st = [p.sem(f"yst{i}") for i in range(2)]
    s_ab = p.sem(t + "ab"); s_abd = p.sem(t + "abd")

    wslab = [p.shared_sbuf(f"wsl{i}", [128, D], BF16) for i in range(2)]
    wsign = [p.shared_sbuf(f"wsg{i}", [128, D], BF16) for i in range(2)]
    wbuf = [p.shared_sbuf(f"wb{i}", [128, 32 * 512], BF16) for i in range(2)]
    ab_sb = p.sbuf(f"{t}_ab", [128, 64], F32)
    evtmp = [p.shared_sbuf(f"ev{i}", [128, 512], F32) for i in range(2)]
    xres = [p.shared_sbuf(f"xr{i}", [128, 512], F32) for i in range(2)]
    yt = [p.shared_sbuf(f"yt{i}", [128, 512], F32) for i in range(2)]
    pp = [p.shared_psum(f"pp{i}", [128, 512]) for i in range(2)]

    sg0, mm20, ev0, dv20 = s_sg.count, s_mm2.count, s_ev.count, s_dv2.count
    p.dma("scalar", ab_sb[:, 0:32], alpha_ap, sem=s_ab)
    ab_v = p.dma("scalar", ab_sb[:, 32:64], bias_ap, sem=s_ab)
    p.op("vector", lambda E: E.tensor_scalar_mul(ab_sb[:, 0:64], ab_sb[:, 0:64], 0.5),
         waits=[(s_ab, ab_v)], sem=s_abd)
    abd_v = s_abd.count

    sw_v = {}
    for i in range(32):
        b = i % 2
        wv_ = p.dma("gpsimd", wslab[b][:], w0t_ap[i * 128:(i + 1) * 128, :],
                    waits=[(s_sg, sg0 + i - 1)], sem=s_wld[b])
        p.op("scalar", lambda E, b=b: E.activation(wsign[b][:], wslab[b][:], AF.Sign),
             waits=[(s_wld[b], wv_), (s_sw[b], sw_v.get(i - 2, 0))],
             sem=s_sg)
        sw_v[i] = p.dma("sync", sw0[i * 128:(i + 1) * 128, :], wsign[b][:],
                        waits=[(s_sg, sg0 + i + 1)], sem=s_sw[b])

    st = emit_ln_stats(p, x_ap, rowbuf, t, want_var=False, hi_lo=True)
    act_bf, s_act, act_base = emit_xln(p, x_ap, st, t, sign=True)

    for oslab in range(8):
        b = oslab % 2
        for k in range(32):
            wl_v = p.dma("sync", wbuf[b][:, k * 512:(k + 1) * 512],
                         sw0[k * 128:(k + 1) * 128, oslab * 512:(oslab + 1) * 512],
                         waits=[(s_sw[k % 2], sw_v[k]),
                                (s_mm2, mm20 + max(0, (oslab - 1) * 256))],
                         sem=s_wld2[b])
        for osub in range(4):
            o = oslab * 4 + osub
            for tb in range(2):
                u = o * 2 + tb
                pb = u % 2
                for k in range(32):
                    p.mm(pp[pb][:], wbuf[b][:, k * 512 + osub * 128: k * 512 + osub * 128 + 128],
                         act_bf[:, k * TPC + tb * 512: k * TPC + tb * 512 + 512],
                         start=(k == 0), stop=(k == 31),
                         waits=[(s_wld2[b], wl_v),
                                (s_act, act_base + k + 1),
                                (s_ev, ev0 + max(0, u - 1))],
                         sem=s_mm2)
                p.op("vector", lambda E, pb=pb, o=o: E.tensor_scalar(
                    evtmp[pb][:], pp[pb][:], ab_sb[:, o:o + 1], ab_sb[:, 32 + o:32 + o + 1],
                    ALU.mult, ALU.add),
                    waits=[(s_mm2, s_mm2.count), (s_dv2, dv20 + max(0, u - 1)),
                           (s_abd, abd_v)],
                    sem=s_ev)
                xr_v = p.dma("scalar", xres[pb][:], x_ap[o * 128:(o + 1) * 128, tb * 512:(tb + 1) * 512],
                             waits=[(s_dv2, dv20 + max(0, u - 1)),
                                    (s_act, act_base + 32)], sem=s_x3[pb])
                p.op("gpsimd", lambda E, pb=pb: E.tensor_tensor(
                    yt[pb][:], evtmp[pb][:], xres[pb][:], ALU.add),
                    waits=[(s_ev, s_ev.count), (s_x3[pb], xr_v),
                           (s_st[pb], s_st[pb].count)],
                    sem=s_dv2)
                p.dma("sync", y_ap[o * 128:(o + 1) * 128, tb * 512:(tb + 1) * 512],
                      yt[pb][:], waits=[(s_dv2, s_dv2.count)], sem=s_st[pb])
    p.wait("sync", (s_st[0], s_st[0].count), (s_st[1], s_st[1].count))
    return s_st[0]


def emit_qkv(p: Prog, x_ap, weights, y_outs, rowbuf, tag, layer,
             rot_tabs=None, battn_ap=None):
    t = tag
    st = emit_ln_stats(p, x_ap, rowbuf, t, want_var=True, hi_lo=False)
    act_bf, s_act, act_base = emit_xln(p, x_ap, st, t, sign=False)

    s_wld = [p.sem(f"gwl{i}") for i in range(2)]
    s_mm = p.sem(t + "mm"); s_ev = p.sem(t + "ev")
    s_sq = p.sem(t + "sq"); s_nm = p.sem(t + "nm"); s_nr = p.sem(t + "nr")
    s_st = [p.sem(t + f"st{i}") for i in range(4)]
    s_tab = p.sem(t + "tab"); s_g = p.sem(t + "g")
    s_tr = p.sem(t + "tr"); s_vst = p.sem(t + "vst")
    s_n2 = [p.sem(t + f"n2{i}") for i in range(2)]

    wbuf = [p.shared_sbuf(f"wb{i}", [128, 32 * 512], BF16) for i in range(2)]
    qk_t = [p.shared_sbuf(f"qk{i}", [128, 512], BF16) for i in range(4)]
    rot = [p.shared_sbuf("rot0", [128, 512], F32)]
    ta = [p.shared_sbuf("ta0", [128, 512], F32)]
    tb_ = [p.shared_sbuf("tb0", [128, 512], F32)]
    sqt = [p.shared_sbuf(f"sq{i}", [128, 512], BF16) for i in range(2)]
    n2r = [p.sbuf(f"{t}_n2r{i}", [1, 512], BF16) for i in range(2)]
    ones128 = p.shared_sbuf("ones128", [128, 1], BF16)
    pp = [p.shared_psum(f"pp{i}", [128, 512]) for i in range(2)]
    ps_n = p.shared_psum("prowB", [1, TPC])[0:1, 0:512]
    if rot_tabs is not None:
        cosq = p.sbuf(f"{t}_cq", [128, TPC], BF16)
        sinq = p.sbuf(f"{t}_sn", [128, TPC], BF16)
        cosk = p.sbuf(f"{t}_ck", [128, TPC], BF16)
        sink = p.sbuf(f"{t}_sk", [128, TPC], BF16)
        for sb, apn in zip((cosq, sinq, cosk, sink), rot_tabs):
            p.dma("gpsimd", sb[:], apn, sem=s_tab)
    if battn_ap is not None:
        nch = sum(len(r[1]) for r in weights)
        bat = p.sbuf(f"{t}_bat", [128, nch], F32)
        p.dma("gpsimd", bat[:], battn_ap, sem=s_tab)
    tab_done = (s_tab, s_tab.count)

    u = 0
    ru = 0
    nunit = 0
    mm0, ev0, g0 = s_mm.count, s_ev.count, s_g.count
    nm0, nr0 = s_nm.count, s_nr.count
    slab_mm_end = {}
    slab_id = 0
    ev_after_unit = {}
    for w_ap, roles in weights:
        nslab = len(roles) // 4
        for oslab in range(nslab):
            b = slab_id % 2
            for k in range(32):
                wl_v = p.dma("gpsimd", wbuf[b][:, k * 512:(k + 1) * 512],
                             w_ap[k * 128:(k + 1) * 128, oslab * 512:(oslab + 1) * 512],
                             waits=[(s_mm, slab_mm_end.get(slab_id - 2, mm0))],
                             sem=s_wld[b])
            for osub in range(4):
                oc = oslab * 4 + osub
                role, head = roles[oc]
                gc = slab_id * 4 + osub
                for tbk in range(2):
                    pb = u % 2
                    for k in range(32):
                        p.mm(pp[pb][:], wbuf[b][:, k * 512 + osub * 128: k * 512 + osub * 128 + 128],
                             act_bf[:, k * TPC + tbk * 512: k * TPC + tbk * 512 + 512],
                             start=(k == 0), stop=(k == 31),
                             waits=[(s_wld[b], wl_v),
                                    (s_act, act_base + k + 1),
                                    (s_ev, ev_after_unit.get(u - 2, ev0))],
                             sem=s_mm)
                    qb = u % 4
                    war_store = (s_st[qb], s_st[qb].count)
                    if rot_tabs is not None and role in ("q", "k"):
                        cs, sn = (cosq, sinq) if role == "q" else (cosk, sink)
                        rb = 0
                        csl = slice(tbk * 512, (tbk + 1) * 512)
                        p.op("vector", lambda E, rb=rb, pb=pb: E.tensor_copy(
                            rot[rb][0:64, :], pp[pb][64:128, :]),
                            waits=[(s_mm, s_mm.count), tab_done,
                                   (s_g, g0 + ru)], sem=s_ev)
                        p.op("vector", lambda E, rb=rb, pb=pb: E.tensor_copy(
                            rot[rb][64:128, :], pp[pb][0:64, :]), sem=s_ev)
                        p.op("vector", lambda E, rb=rb, pb=pb, cs=cs, csl=csl: E.tensor_tensor(
                            ta[rb][:], pp[pb][:], cs[:, csl], ALU.mult),
                            waits=[(s_ev, ev_after_unit.get(u - 1, ev0))], sem=s_ev)
                        p.op("gpsimd", lambda E, rb=rb, sn=sn, csl=csl: E.tensor_tensor(
                            tb_[rb][:], rot[rb][:], sn[:, csl], ALU.mult),
                            waits=[(s_ev, s_ev.count)], sem=s_g)
                        p.op("vector", lambda E, rb=rb, qb=qb: E.tensor_tensor(
                            qk_t[qb][:], ta[rb][:], tb_[rb][:], ALU.add),
                            waits=[(s_g, s_g.count), war_store], sem=s_ev)
                        ru += 1
                    elif role in ("q", "k"):
                        if role == "q":
                            p.op("vector", lambda E, pb=pb, qb=qb, gc=gc: E.tensor_scalar(
                                qk_t[qb][:], pp[pb][:], bat[:, gc:gc + 1], 1.0 / 128.0,
                                ALU.add, ALU.mult),
                                waits=[(s_mm, s_mm.count), tab_done, war_store], sem=s_ev)
                        else:
                            p.op("vector", lambda E, pb=pb, qb=qb, gc=gc: E.tensor_scalar(
                                qk_t[qb][:], pp[pb][:], bat[:, gc:gc + 1], None, ALU.add),
                                waits=[(s_mm, s_mm.count), tab_done, war_store], sem=s_ev)
                    else:
                        if battn_ap is not None:
                            p.op("vector", lambda E, pb=pb, qb=qb, gc=gc: E.tensor_scalar(
                                qk_t[qb][:], pp[pb][:], bat[:, gc:gc + 1], None, ALU.add),
                                waits=[(s_mm, s_mm.count), tab_done, war_store], sem=s_ev)
                        else:
                            p.op("vector", lambda E, pb=pb, qb=qb: E.tensor_copy(
                                qk_t[qb][:], pp[pb][:]),
                                waits=[(s_mm, s_mm.count), war_store], sem=s_ev)
                    ev_after_unit[u] = s_ev.count
                    dst = y_outs["v"] if role == "v" else y_outs[role]
                    p.dma("sync", dst[head * 128:(head + 1) * 128, tbk * 512:(tbk + 1) * 512],
                          qk_t[qb][:], waits=[(s_ev, s_ev.count)], sem=s_st[qb])
                    if role in ("q", "k"):
                        sb2 = nunit % 2
                        p.op("scalar", lambda E, sb2=sb2, qb=qb: E.activation(
                            sqt[sb2][:], qk_t[qb][:], AF.Square),
                            waits=[(s_ev, s_ev.count), (s_nm, nm0 + max(0, nunit - 1))], sem=s_sq)
                        p.mm(ps_n, ones128[:], sqt[sb2][:], start=True, stop=True,
                             waits=[(s_sq, s_sq.count), (s_nr, nr0 + nunit),
                                    (st["s_dv"], st["s_dv"].count)], sem=s_nm)
                        p.op("vector", lambda E, sb2=sb2: E.tensor_copy(
                            n2r[sb2][:], ps_n),
                            waits=[(s_nm, s_nm.count), (s_n2[sb2], s_n2[sb2].count)],
                            sem=s_nr)
                        row = head if role == "q" else 32 + head
                        p.dma("sync", y_outs["n2"][row:row + 1, tbk * 512:(tbk + 1) * 512],
                              n2r[sb2][:], waits=[(s_nr, s_nr.count)], sem=s_n2[sb2])
                        nunit += 1
                    u += 1
            slab_mm_end[slab_id] = s_mm.count
            slab_id += 1

    vtt = p.shared_sbuf("wsl0", [128, D], BF16)
    all_st = [(s, s.count) for s in s_st]
    for tt in range(8):
        tr_v = p.dma_tr("sync", vtt[:], y_outs["v"][:, tt * 128:(tt + 1) * 128],
                        waits=all_st + [(s_vst, s_vst.count)], sem=s_tr)
        p.dma("scalar", y_outs["vt"][tt * 128:(tt + 1) * 128, :], vtt[:],
              waits=[(s_tr, tr_v)], sem=s_vst)
    p.wait("sync", (s_vst, s_vst.count), (s_n2[0], s_n2[0].count),
           (s_n2[1], s_n2[1].count), *[(s, s.count) for s in s_st])
    return s_vst


def emit_attn(p: Prog, qh_ap, kh_ap, vh_ap, n2_ap, masks_ap, o_ap, brow_dram,
              zrow_dram, tag, layer, alt_ap=None, qsl_ap=None, ones2_ap=None,
              gath=None, filler=None, fill_per=0):
    t = tag
    s_ld = p.sem(t + "ld"); s_b = p.sem(t + "b"); s_bs = p.sem(t + "bs")
    s_mm = p.sem(t + "mm"); s_ex = p.sem(t + "ex"); s_mk = p.sem(t + "mk")
    s_q = [p.sem(t + f"q{i}") for i in range(2)]
    s_al = p.sem(t + "al")
    s_alb = [p.sem(t + f"alb{i}") for i in range(2)]
    s_ep = p.sem(t + "ep"); s_zr = p.sem(t + "zr")
    s_zst = [p.sem(t + f"zs{i}") for i in range(2)]
    s_zbb = [p.sem(t + f"zb{i}") for i in range(2)]
    s_st = [p.sem(t + f"st{i}") for i in range(2)]
    s_g = p.sem(t + "g")

    kh_sb = p.shared_sbuf("act_bf", [128, 32 * TPC], BF16)
    vh_sb0 = p.shared_sbuf("wb0", [128, 32 * 512], BF16)
    vh_sb1 = p.shared_sbuf("wb1", [128, 32 * 512], BF16)
    qwin = [p.sbuf(f"{t}_qw{i}", [128, 512], BF16) for i in range(2)]
    pt = [p.sbuf(f"{t}_pt{i}", [128, 512], BF16) for i in range(2)]
    masks = p.sbuf(f"{t}_mk", [128, 2048], BF16)
    n2sb = p.sbuf(f"{t}_n2", [16, S], BF16)
    n2max = p.sbuf(f"{t}_n2m", [16, 1], F32)
    brow = p.sbuf(f"{t}_br", [1, 16], F32)
    negb = p.sbuf(f"{t}_nb", [1, 8], F32)
    negb_cols = p.sbuf(f"{t}_nbc", [128, 8], F32)
    zr = p.sbuf(f"{t}_zr", [1, 512], F32)
    zb = [p.sbuf(f"{t}_zb{i}", [128, 512], F32) for i in range(2)]
    ot = [p.sbuf(f"{t}_ot{i}", [128, 512], BF16) for i in range(2)]
    ones128 = p.shared_sbuf("ones128", [128, 1], BF16)
    ps = [p.shared_psum(f"pp{i}", [128, 512]) for i in range(2)]
    po = [p.shared_psum(f"po{i}", [128, 512]) for i in range(2)]
    prowA = p.shared_psum("prowA", [1, TPC])
    pz = [prowA[0:1, 0:512], prowA[0:1, 512:1024]]
    if layer == 1:
        albuf = [p.sbuf(f"{t}_al{i}", [3, 4096], BF16) for i in range(2)]
        rhs3 = p.sbuf(f"{t}_r3", [3, 4096], BF16)
        qsl = p.sbuf(f"{t}_qs", [1, 4096], BF16)

    mm0, ex0, ep0 = s_mm.count, s_ex.count, s_ep.count
    g_v = p.op("gpsimd", lambda E: E.memset(ones128[:], 1.0), sem=s_g)
    if gath is not None:
        gap, idx_dram, h0col = gath
        s_ix = p.sem("ixs")
        idx_sb = p.sbuf(f"{t}_idx", [128, 132], I32)
        ix_v = p.dma("sync", idx_sb[:], idx_dram, sem=s_ix)
        ixw = (s_ix, ix_v)
    for h in range(8):
        if gath is None:
            ld_v = p.dma("sync", kh_sb[:, h * S:(h + 1) * S],
                         kh_ap[h * 128:(h + 1) * 128, :], sem=s_ld)
        else:
            for r in range(4):
                ld_v = p.idma(kh_sb[:, h * S + r * 1024: h * S + r * 1024 + 1024],
                              gap, idx_sb[:, h * 4 + r: h * 4 + r + 1],
                              waits=[ixw], sem=s_ld)
    for kt in range(32):
        dst = vh_sb0 if kt < 16 else vh_sb1
        co = (kt % 16) * 1024
        if gath is None:
            ld_v = p.dma("sync", dst[:, co:co + 1024],
                         vh_ap[kt * 128:(kt + 1) * 128, :], sem=s_ld)
        else:
            ld_v = p.idma(dst[:, co:co + 1024], gap,
                          idx_sb[:, 32 + kt:33 + kt], waits=[ixw], sem=s_ld)
    s_ldm = p.sem(t + "ldm")
    ldm_v = p.dma("gpsimd", masks[:], masks_ap, sem=s_ldm)
    s_bh = p.sem(t + "bh")
    if gath is None:
        b_v = p.dma("gpsimd", n2sb[:], n2_ap, sem=s_b)
    else:
        for r in range(4):
            b_v = p.idma(n2sb[0:16, r * 1024:(r + 1) * 1024], gap,
                         idx_sb[0:16, 64 + r:65 + r], waits=[ixw], sem=s_b)
    rx_v = p.op("vector", lambda E: E.reduce_max(n2max[:], n2sb[:], axis=mybir.AxisListType.X),
                waits=[(s_b, b_v)], sem=s_bs)
    bh_v = p.dma("sync", brow_dram[0:1, :].rearrange("a (b c) -> (a b) c", c=1),
                 n2max[:], waits=[(s_bs, rx_v)], sem=s_bh)
    bh_v = p.dma("sync", brow[:], brow_dram[0:1, :], waits=[(s_bh, bh_v)], sem=s_bh)
    p.op("vector", lambda E: E.tensor_tensor(negb[:], brow[0:1, 0:8], brow[0:1, 8:16], ALU.mult),
         waits=[(s_bh, bh_v)], sem=s_bs)
    p.op("scalar", lambda E: E.activation(negb[:], negb[:], AF.Sqrt),
         waits=[(s_bs, s_bs.count)], sem=s_bs)
    p.op("vector", lambda E: E.tensor_scalar_mul(negb[:], negb[:], -1.02),
         waits=[(s_bs, s_bs.count)], sem=s_bs)
    nb_done = s_bs.count
    if layer == 0:
        bh_v = p.dma("sync", brow_dram[0:1, 0:8], negb[:], waits=[(s_bs, nb_done)], sem=s_bh)
        p.dma("gpsimd", negb_cols[:], brow_dram[0:1, 0:8].broadcast_to((128, 8)),
              waits=[(s_bh, bh_v)], sem=s_b)
        bias_done = (s_b, s_b.count)
    else:
        p.dma("gpsimd", qsl[:], qsl_ap, sem=s_al)
        al_v0 = p.dma("gpsimd", rhs3[1:3, :], ones2_ap, sem=s_al)
        for h in range(8):
            p.op("vector", lambda E, h=h: E.tensor_scalar(
                rhs3[0:1, h * 512:(h + 1) * 512], qsl[0:1, h * 512:(h + 1) * 512],
                negb[0:1, h:h + 1], None, ALU.add),
                waits=[(s_al, al_v0), (s_bs, nb_done)], sem=s_bs)
        bias_done = (s_bs, s_bs.count)

    u = 0
    ep_after = {}
    pv_after_pt = {}
    alt_base = 0
    for h in range(8):
        for qb in range(8):
            nkt = 4 * (qb + 1)
            qi = h * 8 + qb
            pob = qi % 2
            qw_war = [(s_mm, mm0 if qi < 2 else pv_after_pt.get(("qdone", qi - 2), mm0))]
            if gath is None:
                q_v = p.dma("sync", qwin[qi % 2][:],
                            qh_ap[h * 128:(h + 1) * 128, qb * 512:(qb + 1) * 512],
                            waits=qw_war, sem=s_q[qi % 2])
            else:
                q_v = p.idma(qwin[qi % 2][:], gap,
                             idx_sb[:, 68 + h * 8 + qb: 69 + h * 8 + qb],
                             waits=qw_war + [ixw], sem=s_q[qi % 2],
                             eoff=(qb % 2) * 512)
            if layer == 1:
                al_v = p.dma("sync", albuf[qi % 2][:, 0:nkt * 128],
                             alt_ap[:, alt_base * 128:(alt_base + nkt) * 128],
                             waits=[(s_mm, mm0 if qi < 2 else pv_after_pt.get(("qdone", qi - 2), mm0))],
                             sem=s_alb[qi % 2])
            for kt in range(nkt):
                pb = u % 2
                mmw = [(s_ld, ld_v), (s_q[qi % 2], q_v),
                       (s_ex, ex0 + max(0, u - 1))]
                if layer == 1:
                    p.mm(ps[pb][:], albuf[qi % 2][:, kt * 128:kt * 128 + 128],
                         rhs3[:, h * 512:(h + 1) * 512],
                         start=True, stop=False,
                         waits=mmw + [(s_alb[qi % 2], al_v),
                                      bias_done],
                         sem=s_mm)
                    p.mm(ps[pb][:], kh_sb[:, h * S + kt * 128: h * S + kt * 128 + 128],
                         qwin[qi % 2][:, :], start=False, stop=True, sem=s_mm)
                else:
                    p.mm(ps[pb][:], kh_sb[:, h * S + kt * 128: h * S + kt * 128 + 128],
                         qwin[qi % 2][:, :], start=True, stop=True,
                         waits=mmw, sem=s_mm)
                smm = s_mm.count
                is_diag = (kt >= 4 * qb)
                if is_diag:
                    c = kt - 4 * qb
                    p.op("vector", lambda E, pb=pb, c=c: E.tensor_tensor(
                        ps[pb][:], ps[pb][:], masks[:, c * 512:(c + 1) * 512], ALU.add),
                        waits=[(s_mm, smm), (s_ldm, ldm_v)], sem=s_mk)
                if layer == 0:
                    p.op("scalar", lambda E, pb=pb, h=h: E.activation(
                        pt[pb][:], ps[pb][:], AF.Exp, bias=negb_cols[:, h:h + 1]),
                        waits=[(s_mm, smm), (s_mk, s_mk.count), bias_done,
                               (s_mm, pv_after_pt.get(u - 2, mm0))],
                        sem=s_ex)
                else:
                    p.op("scalar", lambda E, pb=pb: E.activation(
                        pt[pb][:], ps[pb][:], AF.Exp),
                        waits=[(s_mm, smm), (s_mk, s_mk.count),
                               (s_mm, pv_after_pt.get(u - 2, mm0))],
                        sem=s_ex)
                vsl = (vh_sb0 if kt < 16 else vh_sb1)
                vco = (kt % 16) * 1024 + h * 128
                p.mm(po[pob][:], vsl[:, vco:vco + 128], pt[pb][:],
                     start=(kt == 0), stop=(kt == nkt - 1),
                     waits=[(s_ex, s_ex.count),
                            (s_ep, ep_after.get(qi - 2, ep0)),
                            (s_zr, s_zr.count)],
                     sem=s_mm)
                p.mm(pz[pob][:], ones128[:], pt[pb][:],
                     start=(kt == 0), stop=(kt == nkt - 1), sem=s_mm)
                pv_after_pt[u] = s_mm.count
                u += 1
            pv_after_pt[("qdone", qi)] = s_mm.count
            alt_base += nkt
            p.op("vector", lambda E, pob=pob: E.reciprocal(zr[:], pz[pob][0:1, :]),
                 waits=[(s_mm, s_mm.count),
                        (s_zst[1 - pob], s_zst[1 - pob].count)], sem=s_zr)
            zs_v = p.dma("sync", zrow_dram[pob:pob + 1, :], zr[:],
                         waits=[(s_zr, s_zr.count),
                                (s_zbb[pob], s_zbb[pob].count)], sem=s_zst[pob])
            zb_v = p.dma("gpsimd", zb[pob][:],
                         zrow_dram[pob:pob + 1, :].broadcast_to((128, 512)),
                         waits=[(s_zst[pob], zs_v), (s_ep, ep_after.get(qi - 2, ep0))],
                         sem=s_zbb[pob])
            p.op("vector", lambda E, pob=pob: E.tensor_tensor(
                ot[pob][:], po[pob][:], zb[pob][:], ALU.mult),
                waits=[(s_zbb[pob], zb_v), (s_st[pob], s_st[pob].count)], sem=s_ep)
            ep_after[qi] = s_ep.count
            p.dma("sync", o_ap[h * 128:(h + 1) * 128, qb * 512:(qb + 1) * 512],
                  ot[pob][:], waits=[(s_ep, s_ep.count)], sem=s_st[pob])
            if filler is not None:
                for _ in range(fill_per):
                    f = next(filler, None)
                    if f is None:
                        break
                    f()
    p.wait("sync", (s_st[0], s_st[0].count), (s_st[1], s_st[1].count))
    return s_st[0]



def emit_dequant(p: Prog, wspecs, tag):
    t = tag
    s_q = [p.sem(f"swl{i}") for i in range(2)]
    s_s = [p.sem(f"gwl{i}") for i in range(2)]
    s_dv = p.sem(t + "dv")
    s_w = [p.sem(f"yst{i}") for i in range(2)]
    s_z = p.sem(t + "z")
    s_zg = p.sem(t + "zg")
    NMAX = max(w["N"] for w in wspecs)
    N8MAX = NMAX // 8
    qt = [p.sbuf(f"{t}_qt{i}", [128, N8MAX], I32) for i in range(2)]
    sb = [p.sbuf(f"{t}_sb{i}", [128, NMAX], BF16) for i in range(2)]
    ut = [p.sbuf(f"{t}_ut{i}", [128, N8MAX], I32) for i in range(2)]
    wt = [p.sbuf(f"{t}_wt{i}", [128, NMAX], BF16) for i in range(2)]
    gi = 0
    for w in wspecs:
        K_, N, G = w["K_"], w["N"], w["G"]
        N8 = N // 8
        for g in range(G):
            b = gi % 2
            dvw = (s_dv, max(0, (gi - 1) * 16))
            q_v = p.dma("sync", qt[b][0:128, 0:N8], w["qw"][g * 128:(g + 1) * 128, :],
                        waits=[dvw], sem=s_q[b])
            sc_v = p.dma("gpsimd", sb[b][0:128, 0:N], w["sc"][g:g + 1, :].broadcast_to((128, N)),
                         waits=[dvw], sem=s_s[b])
            for s_ in range(8):
                ub = s_ % 2
                p.op("vector", lambda E, b=b, ub=ub, s_=s_, N8=N8: E.tensor_scalar(
                    ut[ub][0:128, 0:N8], qt[b][0:128, 0:N8], 4 * s_, 15,
                    ALU.logical_shift_right, ALU.bitwise_and),
                    waits=[(s_q[b], q_v), (s_s[b], sc_v),
                           (s_w[b], s_w[b].count)],
                    sem=s_dv)
                p.op("vector", lambda E, b=b, ub=ub, s_=s_, N=N, N8=N8: E.tensor_tensor(
                    wt[b][0:128, s_:N:8], ut[ub][0:128, 0:N8], sb[b][0:128, s_:N:8],
                    ALU.mult), sem=s_dv)
            p.dma("sync", w["out"][g * 128:(g + 1) * 128, :], wt[b][0:128, 0:N],
                  waits=[(s_dv, s_dv.count)], sem=s_w[b])
            gi += 1
    for wi, w in enumerate(wspecs):
        K_, N, G = w["K_"], w["N"], w["G"]
        N8 = N // 8
        p.dma("sync", qt[0][0:G, 0:N8], w["qz"], waits=[(s_dv, s_dv.count)], sem=s_z)
        p.dma("gpsimd", sb[0][0:G, 0:N], w["sc"], waits=[(s_dv, s_dv.count)], sem=s_zg)
        p.op("vector", lambda E, G=G, N=N: E.tensor_scalar_mul(
            sb[0][0:G, 0:N], sb[0][0:G, 0:N], -1.0),
            waits=[(s_z, s_z.count), (s_zg, s_zg.count),
                   (s_w[0], s_w[0].count), (s_w[1], s_w[1].count)], sem=s_dv)
        for s_ in range(8):
            p.op("vector", lambda E, G=G, N=N, N8=N8, s_=s_: E.tensor_scalar(
                ut[0][0:G, 0:N8], qt[0][0:G, 0:N8], 4 * s_, 15,
                ALU.logical_shift_right, ALU.bitwise_and), sem=s_dv)
            p.op("vector", lambda E, G=G, N=N, N8=N8, s_=s_: E.tensor_tensor(
                wt[0][0:G, s_:N:8], ut[0][0:G, 0:N8], sb[0][0:G, s_:N:8],
                ALU.mult), sem=s_dv)
        p.dma("sync", w["zs"], wt[0][0:G, 0:N], waits=[(s_dv, s_dv.count)], sem=s_z)
    p.wait("sync", (s_w[0], s_w[0].count), (s_w[1], s_w[1].count), (s_z, s_z.count))


def make_dequant_steps(p: Prog, wspecs, tag):
    t = tag
    s_q = [p.sem(t + f"dq{i}") for i in range(2)]
    s_s = [p.sem(t + f"ds{i}") for i in range(2)]
    s_dv = p.sem(t + "ddv")
    s_w = [p.sem(t + f"dw{i}") for i in range(2)]
    s_z = p.sem(t + "dz")
    s_zg = p.sem(t + "dzg")
    NMAX = max(w["N"] for w in wspecs)
    NH = NMAX // 2
    N8MAX = NMAX // 8
    qt = [p.sbuf(f"{t}_qt{i}", [128, N8MAX], I32) for i in range(2)]
    sb = [p.sbuf(f"{t}_sb{i}", [128, NH], BF16) for i in range(2)]
    ut = [p.sbuf(f"{t}_ut0", [128, N8MAX // 2], I32)] * 2
    wt = [p.sbuf(f"{t}_wt0", [128, NH], BF16)] * 2
    steps = []
    state = dict(gi=0, q_v={}, dvq={})

    def q_load(w, g, gi):
        def fn():
            b = gi % 2
            state["q_v"][gi] = p.dma(
                "sync", qt[b][0:128, 0:w["N"] // 8], w["qw"][g * 128:(g + 1) * 128, :],
                waits=[(s_dv, state["dvq"].get(gi - 2, 0))], sem=s_q[b])
        return fn

    def unit(w, g, gi, hf):
        def fn():
            b = gi % 2
            hb = (2 * gi + hf) % 2
            N = w["N"]
            N8 = N // 8
            NH8 = N8 // 2
            off = hf * (N // 2)
            s_v = p.dma("gpsimd", sb[hb][0:128, 0:N // 2],
                        w["sc"][g:g + 1, off:off + N // 2].broadcast_to((128, N // 2)),
                        waits=[(s_dv, state["dvq"].get(gi - 1, 0))], sem=s_s[hb])
            for s_ in range(8):
                ub = s_ % 2
                p.op("vector", lambda E, b=b, ub=ub, s_=s_, N8=N8, NH8=NH8, hf=hf: E.tensor_scalar(
                    ut[ub][0:128, 0:NH8], qt[b][0:128, hf * NH8:(hf + 1) * NH8], 4 * s_, 15,
                    ALU.logical_shift_right, ALU.bitwise_and),
                    waits=[(s_q[b], state["q_v"][gi]), (s_s[hb], s_v),
                           (s_w[0], s_w[0].count), (s_w[1], s_w[1].count)],
                    sem=s_dv)
                p.op("vector", lambda E, hb=hb, ub=ub, s_=s_, N=N, NH8=NH8: E.tensor_tensor(
                    wt[hb][0:128, s_:N // 2:8], ut[ub][0:128, 0:NH8], sb[hb][0:128, s_:N // 2:8],
                    ALU.mult), sem=s_dv)
            if hf == 1:
                state["dvq"][gi] = s_dv.count
            p.dma("sync", w["out"][g * 128:(g + 1) * 128, off:off + N // 2],
                  wt[hb][0:128, 0:N // 2], waits=[(s_dv, s_dv.count)], sem=s_w[hb])
        return fn

    gi = 0
    for w in wspecs:
        for g in range(w["G"]):
            steps.append(q_load(w, g, gi))
            steps.append(unit(w, g, gi, 0))
            steps.append(unit(w, g, gi, 1))
            gi += 1

    def ztail(w):
        def fn():
            G, N = w["G"], w["N"]
            N8 = N // 8
            p.dma("sync", qt[0][0:G, 0:N8], w["qz"],
                  waits=[(s_dv, s_dv.count)], sem=s_z)
            for hf in range(2):
                off = hf * (N // 2)
                NH8 = N8 // 2
                p.dma("gpsimd", sb[0][0:G, 0:N // 2], w["sc"][0:G, off:off + N // 2],
                      waits=[(s_dv, s_dv.count), (s_zg, s_zg.count)], sem=s_zg)
                p.op("vector", lambda E, G=G, N=N: E.tensor_scalar_mul(
                    sb[0][0:G, 0:N // 2], sb[0][0:G, 0:N // 2], -1.0),
                    waits=[(s_z, s_z.count), (s_zg, s_zg.count),
                           (s_w[0], s_w[0].count), (s_w[1], s_w[1].count)], sem=s_dv)
                for s_ in range(8):
                    p.op("vector", lambda E, G=G, N=N, NH8=NH8, s_=s_, hf=hf: E.tensor_scalar(
                        ut[0][0:G, 0:NH8], qt[0][0:G, hf * NH8:(hf + 1) * NH8], 4 * s_, 15,
                        ALU.logical_shift_right, ALU.bitwise_and), sem=s_dv)
                    p.op("vector", lambda E, G=G, N=N, NH8=NH8, s_=s_: E.tensor_tensor(
                        wt[0][0:G, s_:N // 2:8], ut[0][0:G, 0:NH8], sb[0][0:G, s_:N // 2:8],
                        ALU.mult), sem=s_dv)
                p.dma("sync", w["zs"][0:G, off:off + N // 2], wt[0][0:G, 0:N // 2],
                      waits=[(s_dv, s_dv.count)], sem=s_z)
        return fn

    for w in wspecs:
        steps.append(ztail(w))
    return steps


def emit_ffn1(p: Prog, x_ap, dq, y_ap, rowbuf, xg_dram, hg_dram, tag):
    t = tag
    NI = 11008
    GI = 86
    st = emit_ln_stats(p, x_ap, rowbuf, t, want_var=True, hi_lo=False, nbuf=1)

    s_xl = [p.sem(f"sxl{i}") for i in range(2)]
    s_dvx = p.sem(t + "dvx"); s_act = p.sem(t + "act")
    s_wld = [p.sem(f"swl{i}") for i in range(2)]
    s_mm = p.sem(t + "mm"); s_sl = p.sem(t + "sl")
    s_h = p.sem(t + "h")
    s_xgs = [p.sem(t + f"rs{i}") for i in range(2)]
    s_xgl = p.sem(t + "xgl"); s_xglh = p.sem(t + "xglh")
    s_hgs = [p.sem(t + f"rs{i}") for i in range(2)]
    s_hgl = p.sem(t + "xgl"); s_hglh = p.sem(t + "xglh")
    s_zt = [p.sem(t + f"zt{i}") for i in range(4)]
    s_ev = p.sem(t + "ev")
    s_st = [p.sem(f"yst{i}") for i in range(2)]
    s_xr = [p.sem(f"sxl{i}") for i in range(2)]
    s_re = p.sem(t + "re")

    act = p.at(f"{t}_act", [128, 32 * 512], BF16, SBUF_TOP - 90112 - 32768)
    h_sb = p.at(f"{t}_h", [128, GI * 512], BF16, SBUF_TOP - 90112)
    wb = [p.sbuf(f"{t}_wb{i}", [128, 16 * 512], BF16) for i in range(2)]
    xph = [p.shared_sbuf(f"ev{i}", [128, 512], F32) for i in range(2)]
    tsubh = [p.sbuf(f"{t}_tsh{i}", [128, 512], F32) for i in range(2)]
    xg_f = tsubh[0][0:32, :]
    hg_f = tsubh[1][0:86, :]
    zst = [p.sbuf(f"{t}_zs{i}", [128, 128], BF16) for i in range(4)]
    xg_sb = p.sbuf(f"{t}_xg", [32, 1024], BF16)
    hg_sb = p.sbuf(f"{t}_hg", [GI, 1024], BF16)
    sg = [p.sbuf(f"{t}_sg{i}", [128, 512], BF16) for i in range(2)]
    yt = [p.shared_sbuf(f"yt{i}", [128, 512], F32) for i in range(2)]
    xres = [p.shared_sbuf(f"xr{i}", [128, 512], F32) for i in range(2)]
    rowe = [p.sbuf(f"{t}_re{i}", [1, 512], F32) for i in range(2)]
    ones128 = p.shared_sbuf("ones128", [128, 1], BF16)
    pp = [p.shared_psum(f"pp{i}", [128, 512]) for i in range(2)]
    po = [p.shared_psum(f"po{i}", [128, 512]) for i in range(2)]
    prowA = p.shared_psum("prowA", [1, TPC])
    ps_xg = prowA[0:1, 0:512]
    ps_hg = prowA[0:1, 512:1024]
    mean_b = st["mean_b"]
    rstd_b = st["rstd_b"]

    mm_end_tb = s_mm.count
    for tb in range(2):
        tsl = slice(tb * 512, (tb + 1) * 512)
        act_base = s_act.count
        dvx_base = s_dvx.count
        ev_tb0 = s_ev.count
        for i in range(32):
            b = i % 2
            xl_v = p.dma("scalar", xph[b][:], x_ap[i * 128:(i + 1) * 128, tsl],
                         waits=[(s_dvx, dvx_base + i - 1), (st["s_dv"], st["s_dv"].count),
                                (s_ev, ev_tb0)],
                         sem=s_xl[b])
            p.op("vector", lambda E, b=b, tsl=tsl: E.tensor_tensor(
                tsubh[b][:], xph[b][:], mean_b[:, tsl], ALU.subtract),
                waits=[(s_xl[b], xl_v), st["done"], (s_act, act_base + i - 1),
                       (s_mm, mm_end_tb)],
                sem=s_dvx)
            p.op("gpsimd", lambda E, b=b, i=i, tsl=tsl: E.tensor_tensor(
                act[:, i * 512:(i + 1) * 512], tsubh[b][:], rstd_b[:, tsl], ALU.mult),
                waits=[(s_dvx, s_dvx.count), (s_mm, mm_end_tb)], sem=s_act)
        for g in range(32):
            rb = g % 2
            p.mm(ps_xg, ones128[:], act[:, g * 512:(g + 1) * 512],
                 start=True, stop=True,
                 waits=[(s_act, act_base + g + 1), (s_re, s_re.count)],
                 sem=s_mm)
            p.op("vector", lambda E, rb=rb: E.tensor_copy(rowe[rb][:], ps_xg),
                 waits=[(s_mm, s_mm.count), (s_xgs[rb], s_xgs[rb].count),
                        (s_hgs[rb], s_hgs[rb].count)],
                 sem=s_re)
            p.dma("sync", xg_dram[g:g + 1, :], rowe[rb][:],
                  waits=[(s_re, s_re.count)], sem=s_xgs[rb])
        xgall = [(s, s.count) for s in s_xgs]
        p.dma("gpsimd", xg_sb[:, 0:512], xg_dram.ap(),
              waits=xgall + [(s_mm, mm_end_tb)], sem=s_xgl)
        p.dma("scalar", xg_f, xg_dram.ap(),
              waits=xgall, sem=s_xglh)
        p.op("vector", lambda E: E.tensor_tensor(
            xg_sb[0:32, 512:1024], xg_f, xg_sb[0:32, 0:512], ALU.subtract),
            waits=[(s_xgl, s_xgl.count), (s_xglh, s_xglh.count),
                   (s_mm, mm_end_tb)], sem=s_dvx)
        xgl = (s_dvx, s_dvx.count)
        xgl_done = 0

        h_base = s_h.count
        pair_mm = {}
        khalf_mm = {}
        for np_ in range(43):
            g0, g1 = np_ * 2, np_ * 2 + 1
            for c2 in range(2):
                g = np_ * 2 + c2
                zi = c2 * 2
                p.dma("gpsimd", zst[zi][0:32, :], dq["zsg"][0:32, g * 128:(g + 1) * 128],
                      waits=[(s_mm, pair_mm.get(np_ - 1, mm_end_tb))], sem=s_zt[zi])
                p.dma("gpsimd", zst[zi + 1][0:32, :], dq["zsu"][0:32, g * 128:(g + 1) * 128],
                      sem=s_zt[zi + 1])
            corr_w = [(s_zt[0], s_zt[0].count), (s_zt[1], s_zt[1].count),
                      (s_zt[2], s_zt[2].count), (s_zt[3], s_zt[3].count),
                      xgl, (s_act, act_base + 32),
                      (s_sl, s_sl.count), (s_h, s_h.count)]
            for c2 in range(2):
                p.mm(pp[c2][:], zst[c2 * 2][0:32, :], xg_sb[0:32, 0:512], start=True, stop=False,
                     waits=corr_w, sem=s_mm)
                p.mm(pp[c2][:], zst[c2 * 2][0:32, :], xg_sb[0:32, 512:1024], start=False, stop=False,
                     sem=s_mm)
                p.mm(po[c2][:], zst[c2 * 2 + 1][0:32, :], xg_sb[0:32, 0:512], start=True, stop=False,
                     sem=s_mm)
                p.mm(po[c2][:], zst[c2 * 2 + 1][0:32, :], xg_sb[0:32, 512:1024], start=False, stop=False,
                     sem=s_mm)
            for kh in range(2):
                b = kh
                ksl = slice(kh * 16 * 128, (kh + 1) * 16 * 128)
                p.dma("sync", wb[b][:, 0:16 * 256].rearrange("p (k c) -> p k c", c=256),
                      dq["g"][ksl, np_ * 256:(np_ + 1) * 256].rearrange(
                          "(k p) c -> p k c", p=128),
                      waits=[(s_mm, khalf_mm.get((np_ - 1, kh), mm_end_tb))], sem=s_wld[b])
                p.dma("sync", wb[b][:, 16 * 256:16 * 512].rearrange("p (k c) -> p k c", c=256),
                      dq["u"][ksl, np_ * 256:(np_ + 1) * 256].rearrange(
                          "(k p) c -> p k c", p=128),
                      sem=s_wld[b])
                wld_here = (s_wld[b], s_wld[b].count)
                for c2 in range(2):
                    for k in range(16):
                        kk = kh * 16 + k
                        p.mm(pp[c2][:], wb[b][:, k * 256 + c2 * 128: k * 256 + c2 * 128 + 128],
                             act[:, kk * 512:(kk + 1) * 512],
                             start=False, stop=(kk == 31),
                             waits=[wld_here], sem=s_mm)
                    for k in range(16):
                        kk = kh * 16 + k
                        p.mm(po[c2][:], wb[b][:, 16 * 256 + k * 256 + c2 * 128:
                                              16 * 256 + k * 256 + c2 * 128 + 128],
                             act[:, kk * 512:(kk + 1) * 512],
                             start=False, stop=(kk == 31), sem=s_mm)
                khalf_mm[(np_, kh)] = s_mm.count
            allk_mm = s_mm.count
            for c2 in range(2):
                g = np_ * 2 + c2
                p.op("scalar", lambda E, c2=c2, g=g: E.activation(
                    sg[g % 2][:], pp[c2][:], AF.Silu),
                    waits=[(s_mm, allk_mm), (s_h, max(0, s_h.count - 1))], sem=s_sl)
                p.op("vector", lambda E, c2=c2, g=g: E.scalar_tensor_tensor(
                    h_sb[:, g * 512:(g + 1) * 512], sg[g % 2][:], 0.5, po[c2][:],
                    ALU.mult, ALU.mult),
                    waits=[(s_sl, s_sl.count)], sem=s_h)
                rb = g % 2
                p.mm(ps_hg, ones128[:], h_sb[:, g * 512:(g + 1) * 512],
                     start=True, stop=True,
                     waits=[(s_h, s_h.count), (s_re, s_re.count)], sem=s_mm)
                p.op("vector", lambda E, rb=rb: E.tensor_copy(rowe[rb][:], ps_hg),
                     waits=[(s_mm, s_mm.count), (s_hgs[rb], s_hgs[rb].count),
                            (s_xgs[rb], s_xgs[rb].count)],
                     sem=s_re)
                p.dma("sync", hg_dram[g:g + 1, :], rowe[rb][:],
                      waits=[(s_re, s_re.count)], sem=s_hgs[rb])
            pair_mm[np_] = s_mm.count
        gate_end = s_mm.count
        hgall = [(s, s.count) for s in s_hgs]
        p.dma("gpsimd", hg_sb[:, 0:512], hg_dram.ap(),
              waits=hgall + [(s_mm, mm_end_tb)], sem=s_hgl)
        p.dma("scalar", hg_f, hg_dram.ap(),
              waits=hgall, sem=s_hglh)
        p.op("vector", lambda E: E.tensor_tensor(
            hg_sb[0:GI, 512:1024], hg_f, hg_sb[0:GI, 0:512], ALU.subtract),
            waits=[(s_hgl, s_hgl.count), (s_hglh, s_hglh.count),
                   (s_mm, gate_end)], sem=s_dvx)
        hgl = (s_dvx, s_dvx.count)

        dhalf_mm = {}
        for osl in range(32):
            pu = osl % 2
            zi = 2 + (osl % 2)
            war = (s_mm, max(dhalf_mm.get((osl - 1, 0), gate_end),
                             dhalf_mm.get((osl - 2, 1), 0)))
            zt_v = p.dma("gpsimd", zst[zi][0:GI, :], dq["zsd"][0:GI, osl * 128:(osl + 1) * 128],
                         waits=[war], sem=s_zt[zi])
            p.mm(po[pu][:], zst[zi][0:GI, :], hg_sb[0:GI, 0:512], start=True, stop=False,
                 waits=[(s_zt[zi], zt_v), hgl,
                        (s_ev, max(0, s_ev.count - 1)), (s_h, s_h.count)],
                 sem=s_mm)
            p.mm(po[pu][:], zst[zi][0:GI, :], hg_sb[0:GI, 512:1024], start=False, stop=False,
                 sem=s_mm)
            for ghalf in range(2):
                b2 = ghalf
                gsl = slice(ghalf * 43 * 128, (ghalf + 1) * 43 * 128)
                dw_v = p.dma("sync", wb[b2][:, 0:43 * 128].rearrange("q (g c) -> q g c", c=128),
                             dq["d"][gsl, osl * 128:(osl + 1) * 128].rearrange(
                                 "(g q) c -> q g c", q=128),
                             waits=[(s_mm, dhalf_mm.get((osl - 1, ghalf), gate_end))],
                             sem=s_wld[b2])
                for gg in range(43):
                    g = ghalf * 43 + gg
                    p.mm(po[pu][:], wb[b2][:, gg * 128:(gg + 1) * 128],
                         h_sb[:, g * 512:(g + 1) * 512],
                         start=False, stop=(g == GI - 1),
                         waits=[(s_wld[b2], dw_v), (s_h, h_base + g + 1)], sem=s_mm)
                dhalf_mm[(osl, ghalf)] = s_mm.count
            xr_v = p.dma("scalar", xres[pu][:], x_ap[osl * 128:(osl + 1) * 128, tsl],
                         waits=[(s_ev, max(0, s_ev.count - 1)),
                                (s_act, act_base + 32)], sem=s_xr[pu])
            p.op("vector", lambda E, pu=pu: E.tensor_tensor(
                yt[pu][:], po[pu][:], xres[pu][:], ALU.add),
                waits=[(s_mm, s_mm.count), (s_xr[pu], xr_v),
                       (s_st[pu], s_st[pu].count)],
                sem=s_ev)
            p.dma("sync", y_ap[osl * 128:(osl + 1) * 128, tsl], yt[pu][:],
                  waits=[(s_ev, s_ev.count)], sem=s_st[pu])
        mm_end_tb = s_mm.count
    p.wait("sync", (s_st[0], s_st[0].count), (s_st[1], s_st[1].count))
    return s_st[0]


def emit_linres(p: Prog, actsrc_ap, w_ap, x_ap, y_ap, tag, bias_ap=None,
                act_gath=None):
    t = tag
    s_a = p.sem(t + "a")
    s_wld = [p.sem(f"gwl{i}") for i in range(2)]
    s_mm = p.sem(t + "mm")
    s_ev = p.sem(t + "ev")
    s_x3 = [p.sem(f"sxl{i}") for i in range(2)]
    s_dv2 = p.sem(t + "dv2")
    s_st = [p.sem(f"yst{i}") for i in range(2)]
    s_b = p.sem(t + "b")

    act_bf = p.shared_sbuf("act_bf", [128, 32 * TPC], BF16)
    wbuf = [p.shared_sbuf(f"wb{i}", [128, 32 * 512], BF16) for i in range(2)]
    evtmp = [p.shared_sbuf(f"ev{i}", [128, 512], F32) for i in range(2)]
    xres = [p.shared_sbuf(f"xr{i}", [128, 512], F32) for i in range(2)]
    yt = [p.shared_sbuf(f"yt{i}", [128, 512], F32) for i in range(2)]
    pp = [p.shared_psum(f"pp{i}", [128, 512]) for i in range(2)]
    mm0, ev0, dv20, a0 = s_mm.count, s_ev.count, s_dv2.count, s_a.count
    if bias_ap is not None:
        bia = p.sbuf(f"{t}_bia", [128, 32], F32)
        p.dma("scalar", bia[:], bias_ap, sem=s_b)
    bias_done = (s_b, s_b.count)

    if act_gath is not None:
        gap2, idx2_dram = act_gath
        s_ix2 = p.sem("ixs")
        idx2_sb = p.sbuf(f"{t}_idx2", [128, 32], I32)
        ix2_v = p.dma("sync", idx2_sb[:], idx2_dram, sem=s_ix2)
    for i in range(32):
        if act_gath is None:
            a_all = p.dma("sync", act_bf[:, i * TPC:(i + 1) * TPC],
                          actsrc_ap[i * 128:(i + 1) * 128, :], sem=s_a)
        else:
            a_all = p.idma(act_bf[:, i * TPC:(i + 1) * TPC], gap2,
                           idx2_sb[:, i:i + 1],
                           waits=[(s_ix2, ix2_v)], sem=s_a)
    for oslab in range(8):
        b = oslab % 2
        for k in range(32):
            wl_v = p.dma("gpsimd", wbuf[b][:, k * 512:(k + 1) * 512],
                         w_ap[k * 128:(k + 1) * 128, oslab * 512:(oslab + 1) * 512],
                         waits=[(s_mm, mm0 + max(0, (oslab - 1) * 256))],
                         sem=s_wld[b])
        for osub in range(4):
            o = oslab * 4 + osub
            for tb in range(2):
                u = o * 2 + tb
                pb = u % 2
                for k in range(32):
                    p.mm(pp[pb][:], wbuf[b][:, k * 512 + osub * 128: k * 512 + osub * 128 + 128],
                         act_bf[:, k * TPC + tb * 512: k * TPC + tb * 512 + 512],
                         start=(k == 0), stop=(k == 31),
                         waits=[(s_wld[b], wl_v),
                                (s_a, a_all),
                                (s_ev, ev0 + max(0, u - 1))],
                         sem=s_mm)
                if bias_ap is not None:
                    p.op("vector", lambda E, pb=pb, o=o: E.tensor_scalar(
                        evtmp[pb][:], pp[pb][:], bia[:, o:o + 1], None, ALU.add),
                        waits=[(s_mm, s_mm.count), (s_dv2, dv20 + max(0, u - 1)), bias_done],
                        sem=s_ev)
                else:
                    p.op("vector", lambda E, pb=pb: E.tensor_copy(
                        evtmp[pb][:], pp[pb][:]),
                        waits=[(s_mm, s_mm.count), (s_dv2, dv20 + max(0, u - 1))],
                        sem=s_ev)
                xr_v = p.dma("scalar", xres[pb][:], x_ap[o * 128:(o + 1) * 128, tb * 512:(tb + 1) * 512],
                             waits=[(s_dv2, dv20 + max(0, u - 1))], sem=s_x3[pb])
                p.op("gpsimd", lambda E, pb=pb: E.tensor_tensor(
                    yt[pb][:], evtmp[pb][:], xres[pb][:], ALU.add),
                    waits=[(s_ev, s_ev.count), (s_x3[pb], xr_v),
                           (s_st[pb], s_st[pb].count)],
                    sem=s_dv2)
                p.dma("sync", y_ap[o * 128:(o + 1) * 128, tb * 512:(tb + 1) * 512],
                      yt[pb][:], waits=[(s_dv2, s_dv2.count)], sem=s_st[pb])
    p.wait("sync", (s_st[0], s_st[0].count), (s_st[1], s_st[1].count))
    return s_st[0]



def shard_tokens_fm(x):
    flat = np.ascontiguousarray(x.reshape(8192, D))
    return [np.ascontiguousarray(flat[c * TPC:(c + 1) * TPC].T) for c in range(NCORES)]


def unshard_tokens_fm(shards):
    flat = np.concatenate([s.T for s in shards], axis=0)
    return np.ascontiguousarray(flat.reshape(2, S, D))


def chunkmajor(v):
    return np.ascontiguousarray(v.reshape(32, 128).T)


def _run(nc, in_maps):
    res = run_bass_kernel_spmd(nc, in_maps, core_ids=list(range(NCORES)))
    return res.results


def host_masks():
    ki = np.arange(128)[:, None]
    qi = np.arange(512)[None, :]
    m = np.zeros((128, 2048), dtype=np.float64)
    for c in range(4):
        m[:, c * 512:(c + 1) * 512] = np.where(128 * c + ki <= qi, 0.0, -1e9)
    return np.ascontiguousarray(m.astype(NP_BF16))


def rot_tables(core, scale):
    pos = (core % 4) * 1024 + np.arange(TPC, dtype=np.float64)
    invf = 1.0 / 10000.0 ** (np.arange(0, HD, 2, dtype=np.float64) / HD)
    d = np.arange(128)
    ang = invf[d % 64][:, None] * pos[None, :]
    cos = np.cos(ang) * scale
    sin = np.sin(ang) * scale
    sgn = np.where(d < 64, -1.0, 1.0)[:, None]
    return (np.ascontiguousarray(cos.astype(NP_BF16)),
            np.ascontiguousarray((sin * sgn).astype(NP_BF16)))


def alibi_slopes(n_heads=32):
    cp2 = 2 ** math.floor(math.log2(n_heads))
    base = 2 ** (-(2 ** (-(math.log2(cp2) - 3))))
    slopes = base ** np.arange(1, cp2 + 1, dtype=np.float64)
    if cp2 != n_heads:
        extra_base = 2 ** (-(2 ** (-(math.log2(2 * cp2) - 3))))
        n_rem = min(cp2, n_heads - cp2)
        extra = extra_base ** np.arange(1, 1 + 2 * n_rem, 2, dtype=np.float64)
        slopes = np.concatenate([slopes, extra])
    return slopes


def alibi_tables(core):
    sl = alibi_slopes()
    h0 = (core % 4) * 8
    ki = np.arange(128, dtype=np.float64)
    cols = []
    for hp in range(8):
        s = sl[h0 + hp]
        for qb in range(8):
            for kt in range(4 * (qb + 1)):
                r1 = s * (ki + kt * 128 - qb * 512)
                hi = r1.astype(np.float32).astype(NP_BF16).astype(np.float64)
                lo = (r1 - hi).astype(np.float32).astype(NP_BF16)
                cols.append(np.stack([np.ones(128, np.float32).astype(NP_BF16),
                                      hi.astype(NP_BF16), lo]))
    alT = np.ascontiguousarray(np.concatenate(cols, axis=1).astype(NP_BF16))
    qi = np.arange(512, dtype=np.float64)
    qsl = np.concatenate([(-sl[h0 + hp] * qi) for hp in range(8)])
    qsl = np.ascontiguousarray(qsl.astype(np.float32).reshape(1, 4096).astype(NP_BF16))
    ones2 = np.ones((2, 4096), dtype=np.float32).astype(NP_BF16)
    return alT, qsl, ones2


def _build_ffn0():
    p = Prog("ffn0")
    x_ap = p.dram("x", [D, TPC], F32, kind="ExternalInput").ap()
    w0t_ap = p.dram("w0t", [D, D], F32, kind="ExternalInput").ap()
    alpha_ap = p.dram("alpha", [128, 32], F32, kind="ExternalInput").ap()
    bias_ap = p.dram("bias", [128, 32], F32, kind="ExternalInput").ap()
    y_ap = p.dram("y", [D, TPC], F32, kind="ExternalOutput").ap()
    sw0 = p.dram("sw0", [D, D], BF16)
    rowbuf = p.dram("rowbuf", [2, TPC], F32)
    nc = p.build(lambda p: emit_ffn0(p, x_ap, w0t_ap, alpha_ap, bias_ap, y_ap,
                                     sw0.ap(), rowbuf, "f0"))
    return nc


def _build_qkv(layer):
    p = Prog(f"qkv{layer}")
    x_ap = p.dram("x", [D, TPC], F32, kind="ExternalInput").ap()
    q_ap = p.dram("q", [D, TPC], BF16, kind="ExternalOutput").ap()
    k_ap = p.dram("k", [D, TPC], BF16, kind="ExternalOutput").ap()
    vt_ap = p.dram("vt", [TPC, D], BF16, kind="ExternalOutput").ap()
    n2_ap = p.dram("n2", [64, TPC], BF16, kind="ExternalOutput").ap()
    vfm = p.dram("vfm", [D, TPC], BF16)
    rowbuf = p.dram("rowbuf", [2, TPC], F32)
    y_outs = dict(q=q_ap, k=k_ap, v=vfm.ap(), vt=vt_ap, n2=n2_ap)
    if layer == 0:
        wq = p.dram("wq", [D, D], F32, kind="ExternalInput").ap()
        wk = p.dram("wk", [D, D], F32, kind="ExternalInput").ap()
        wv = p.dram("wv", [D, D], F32, kind="ExternalInput").ap()
        tabs = [p.dram(n, [128, TPC], BF16, kind="ExternalInput").ap()
                for n in ("cq", "sq", "ck", "sk")]
        weights = [(wq, [("q", h) for h in range(32)]),
                   (wk, [("k", h) for h in range(32)]),
                   (wv, [("v", h) for h in range(32)])]
        nc = p.build(lambda p: emit_qkv(p, x_ap, weights, y_outs, rowbuf, "qk", 0,
                                        rot_tabs=tabs))
    else:
        wa = p.dram("wa", [D, 3 * D], F32, kind="ExternalInput").ap()
        ba = p.dram("ba", [128, 96], F32, kind="ExternalInput").ap()
        roles = ([("q", h) for h in range(32)] + [("k", h) for h in range(32)]
                 + [("v", h) for h in range(32)])
        weights = [(wa, roles)]
        nc = p.build(lambda p: emit_qkv(p, x_ap, weights, y_outs, rowbuf, "qk", 1,
                                        battn_ap=ba))
    return nc


def _build_attn(layer):
    p = Prog(f"attn{layer}")
    qh_ap = p.dram("qh", [1024, S], BF16, kind="ExternalInput").ap()
    kh_ap = p.dram("kh", [1024, S], BF16, kind="ExternalInput").ap()
    vh_ap = p.dram("vh", [S, 1024], BF16, kind="ExternalInput").ap()
    n2_ap = p.dram("n2", [16, S], BF16, kind="ExternalInput").ap()
    mk_ap = p.dram("mk", [128, 2048], BF16, kind="ExternalInput").ap()
    o_ap = p.dram("o", [1024, S], BF16, kind="ExternalOutput").ap()
    browd = p.dram("browd", [1, 16], F32)
    zrowd = p.dram("zrowd", [2, 512], F32)
    if layer == 0:
        nc = p.build(lambda p: emit_attn(p, qh_ap, kh_ap, vh_ap, n2_ap, mk_ap,
                                         o_ap, browd, zrowd, "at", 0))
    else:
        alt_ap = p.dram("alt", [3, 1152 * 128], BF16, kind="ExternalInput").ap()
        qsl_ap = p.dram("qsl", [1, 4096], BF16, kind="ExternalInput").ap()
        on2_ap = p.dram("on2", [2, 4096], BF16, kind="ExternalInput").ap()
        nc = p.build(lambda p: emit_attn(p, qh_ap, kh_ap, vh_ap, n2_ap, mk_ap,
                                         o_ap, browd, zrowd, "at", 1,
                                         alt_ap=alt_ap, qsl_ap=qsl_ap,
                                         ones2_ap=on2_ap))
    return nc


def _build_proj():
    p = Prog("proj")
    a_ap = p.dram("a", [D, TPC], BF16, kind="ExternalInput").ap()
    w_ap = p.dram("w", [D, D], F32, kind="ExternalInput").ap()
    x_ap = p.dram("x", [D, TPC], F32, kind="ExternalInput").ap()
    b_ap = p.dram("b", [128, 32], F32, kind="ExternalInput").ap()
    y_ap = p.dram("y", [D, TPC], F32, kind="ExternalOutput").ap()
    nc = p.build(lambda p: emit_linres(p, a_ap, w_ap, x_ap, y_ap, "pr", bias_ap=b_ap))
    return nc


def _build_deq():
    p = Prog("deq")
    def din(n, sh, dt):
        return p.dram(n, sh, dt, kind="ExternalInput").ap()
    wspecs = [
        dict(qw=din("gq", [4096, 1376], I32), sc=din("gs", [32, 11008], F32),
             qz=din("gz", [32, 1376], I32),
             out=p.dram("dg", [4096, 11008], BF16, kind="ExternalOutput").ap(),
             zs=p.dram("zg", [32, 11008], BF16, kind="ExternalOutput").ap(),
             K_=4096, N=11008, G=32),
        dict(qw=din("uq", [4096, 1376], I32), sc=din("us", [32, 11008], F32),
             qz=din("uz", [32, 1376], I32),
             out=p.dram("du", [4096, 11008], BF16, kind="ExternalOutput").ap(),
             zs=p.dram("zu", [32, 11008], BF16, kind="ExternalOutput").ap(),
             K_=4096, N=11008, G=32),
        dict(qw=din("dq", [11008, 512], I32), sc=din("ds", [86, 4096], F32),
             qz=din("dz", [86, 512], I32),
             out=p.dram("dd", [11008, 4096], BF16, kind="ExternalOutput").ap(),
             zs=p.dram("zd", [86, 4096], BF16, kind="ExternalOutput").ap(),
             K_=11008, N=4096, G=86),
    ]
    nc = p.build(lambda p: emit_dequant(p, wspecs, "dq"))
    return nc


def _build_ffn1():
    p = Prog("ffn1c")
    x_ap = p.dram("x", [D, TPC], F32, kind="ExternalInput").ap()
    dqd = dict(
        g=p.dram("dg", [4096, 11008], BF16, kind="ExternalInput").ap(),
        u=p.dram("du", [4096, 11008], BF16, kind="ExternalInput").ap(),
        d=p.dram("dd", [11008, 4096], BF16, kind="ExternalInput").ap(),
        zsg=p.dram("zg", [32, 11008], BF16, kind="ExternalInput").ap(),
        zsu=p.dram("zu", [32, 11008], BF16, kind="ExternalInput").ap(),
        zsd=p.dram("zd", [86, 4096], BF16, kind="ExternalInput").ap(),
    )
    y_ap = p.dram("y", [D, TPC], F32, kind="ExternalOutput").ap()
    rowbuf = p.dram("rowbuf", [2, TPC], F32)
    xg_d = p.dram("xgd", [32, 512], F32)
    hg_d = p.dram("hgd", [86, 512], F32)
    nc = p.build(lambda p: emit_ffn1(p, x_ap, dqd, y_ap, rowbuf, xg_d, hg_d, "f1"))
    return nc


def _to_headshard(res, key_q="q", key_k="k", key_vt="vt", key_n2="n2"):
    out = []
    for c in range(NCORES):
        b, h0 = c // 4, (c % 4) * 8
        js = range(b * 4, b * 4 + 4)
        qf = np.concatenate([res[j][key_q] for j in js], axis=1)
        kf = np.concatenate([res[j][key_k] for j in js], axis=1)
        vtf = np.concatenate([res[j][key_vt] for j in js], axis=0)
        n2f = np.concatenate([res[j][key_n2] for j in js], axis=1)
        out.append(dict(
            qh=np.ascontiguousarray(qf[h0 * 128:(h0 + 8) * 128]),
            kh=np.ascontiguousarray(kf[h0 * 128:(h0 + 8) * 128]),
            vh=np.ascontiguousarray(vtf[:, h0 * 128:(h0 + 8) * 128]),
            n2=np.ascontiguousarray(np.concatenate(
                [n2f[h0:h0 + 8], n2f[32 + h0:32 + h0 + 8]], axis=0)),
        ))
    return out


def _from_headshard(res, key="o"):
    out = []
    for c in range(NCORES):
        b, j = c // 4, c % 4
        full = np.concatenate([res[cc][key] for cc in range(b * 4, b * 4 + 4)], axis=0)
        out.append(np.ascontiguousarray(full[:, j * 1024:(j + 1) * 1024]))
    return out


RB1 = 12352
RB2 = 4096
RG = [[0, 1, 2, 3], [4, 5, 6, 7]]


def host_idx(core):
    c4 = core % 4
    h0 = c4 * 8
    ar = np.arange(128, dtype=np.int64)
    cols = []
    for l in range(2):
        for h in range(8):
            for r in range(4):
                cols.append(r * RB1 + (h0 + h) * 128 + ar)
        for kt in range(32):
            r, tl = kt // 8, kt % 8
            cols.append(r * RB1 + 8192 + (tl * 128 + ar) * 4 + c4)
        for r in range(4):
            v = np.zeros(128, np.int64)
            idx16 = np.concatenate([h0 + np.arange(8), 32 + h0 + np.arange(8)])
            v[:16] = r * RB1 + 12288 + idx16
            cols.append(v)
        for h in range(8):
            for qb in range(8):
                r = qb // 2
                cols.append(r * RB1 + 4096 + (h0 + h) * 128 + ar)
        for i in range(32):
            r = i // 8
            cols.append(r * RB2 + ((i % 8) * 128 + ar) * 4 + c4)
    return np.ascontiguousarray(np.stack(cols, axis=1).astype(np.int32))


def _build_fused():
    p = Prog("fused")
    x_in = p.dram("x", [D, TPC], F32, kind="ExternalInput").ap()
    y_out = p.dram("y", [D, TPC], F32, kind="ExternalOutput").ap()
    w0t_ap = p.dram("w0t", [D, D], F32, kind="ExternalInput").ap()
    alpha_ap = p.dram("alpha", [128, 32], F32, kind="ExternalInput").ap()
    bias_ap = p.dram("bias", [128, 32], F32, kind="ExternalInput").ap()
    wq = p.dram("wq", [D, D], F32, kind="ExternalInput").ap()
    wk = p.dram("wk", [D, D], F32, kind="ExternalInput").ap()
    wv = p.dram("wv", [D, D], F32, kind="ExternalInput").ap()
    wo = p.dram("wo", [D, D], F32, kind="ExternalInput").ap()
    wa = p.dram("wa", [D, 3 * D], F32, kind="ExternalInput").ap()
    ba = p.dram("ba", [128, 96], F32, kind="ExternalInput").ap()
    wp = p.dram("wp", [D, D], F32, kind="ExternalInput").ap()
    bp = p.dram("bp", [128, 32], F32, kind="ExternalInput").ap()
    tabs = [p.dram(n, [128, TPC], BF16, kind="ExternalInput").ap()
            for n in ("cq", "sq", "ck", "sk")]
    mk_ap = p.dram("mk", [128, 2048], BF16, kind="ExternalInput").ap()
    alt_ap = p.dram("alt", [3, 1152 * 128], BF16, kind="ExternalInput").ap()
    qsl_ap = p.dram("qsl", [1, 4096], BF16, kind="ExternalInput").ap()
    on2_ap = p.dram("on2", [2, 4096], BF16, kind="ExternalInput").ap()
    idx_ap = p.dram("idx", [128, 328], I32, kind="ExternalInput").ap()
    qin = {}
    for n, sh, dt in [("gq", [4096, 1376], I32), ("gs", [32, 11008], F32),
                      ("gz", [32, 1376], I32), ("uq", [4096, 1376], I32),
                      ("us", [32, 11008], F32), ("uz", [32, 1376], I32),
                      ("dq", [11008, 512], I32), ("ds", [86, 4096], F32),
                      ("dz", [86, 512], I32)]:
        qin[n] = p.dram(n, sh, dt, kind="ExternalInput").ap()

    XA = p.dram("XA", [D, TPC], F32)
    XB = p.dram("XB", [D, TPC], F32)
    sw0 = p.dram("sw0", [D, D], BF16)
    rowbuf = p.dram("rowbuf", [2, TPC], F32)
    xg_d = p.dram("xgd", [32, 512], F32)
    hg_d = p.dram("hgd", [86, 512], F32)
    browd = p.dram("browd", [1, 16], F32)
    zrowd = p.dram("zrowd", [2, 512], F32)
    dg = p.dram("dg", [4096, 11008], BF16)
    du = p.dram("du", [4096, 11008], BF16)
    dd = p.dram("dd", [11008, 4096], BF16)
    zg = p.dram("zg", [32, 11008], BF16)
    zu = p.dram("zu", [32, 11008], BF16)
    zd = p.dram("zd", [86, 4096], BF16)
    g1in = p.dram("g1in", [RB1, 1024], BF16)
    g1out = p.dram("g1out", [4 * RB1, 1024], BF16)
    g2in = p.dram("g2in", [RB2, 1024], BF16)
    g2out = p.dram("g2out", [4 * RB2, 1024], BF16)

    wspecs = [
        dict(qw=qin["gq"], sc=qin["gs"], qz=qin["gz"], out=dg.ap(), zs=zg.ap(),
             K_=4096, N=11008, G=32),
        dict(qw=qin["uq"], sc=qin["us"], qz=qin["uz"], out=du.ap(), zs=zu.ap(),
             K_=4096, N=11008, G=32),
        dict(qw=qin["dq"], sc=qin["ds"], qz=qin["dz"], out=dd.ap(), zs=zd.ap(),
             K_=11008, N=4096, G=86),
    ]
    dqd = dict(g=dg.ap(), u=du.ap(), d=dd.ap(), zsg=zg.ap(), zsu=zu.ap(), zsd=zd.ap())
    y_outs = dict(
        q=g1in.ap()[4096:8192, :],
        k=g1in.ap()[0:4096, :],
        v=p.dram("vfm", [D, TPC], BF16).ap(),
        vt=g1in.ap()[8192:12288, :].rearrange("(t r) c -> t (r c)", r=4),
        n2=g1in.ap()[12288:12352, :],
    )

    def emit(p):
        with p.scope():
            emit_ffn0(p, x_in, w0t_ap, alpha_ap, bias_ap, XA.ap(), sw0.ap(),
                      rowbuf, "f0")
        p.barrier()
        for layer in range(2):
            xcur = XA if layer == 0 else XB
            xnext = XB if layer == 0 else XA
            with p.scope():
                if layer == 0:
                    weights = [(wq, [("q", h) for h in range(32)]),
                               (wk, [("k", h) for h in range(32)]),
                               (wv, [("v", h) for h in range(32)])]
                    emit_qkv(p, xcur.ap(), weights, y_outs, rowbuf, "qk", 0,
                             rot_tabs=tabs)
                else:
                    roles = ([("q", h) for h in range(32)] + [("k", h) for h in range(32)]
                             + [("v", h) for h in range(32)])
                    emit_qkv(p, xcur.ap(), [(wa, roles)], y_outs, rowbuf, "qk", 1,
                             battn_ap=ba)
            p.barrier()
            s_cc = p.sem("cc1")
            p.op("gpsimd", lambda E: E.collective_compute(
                "AllGather", ALU.bypass, replica_groups=RG,
                ins=[g1in.ap()], outs=[g1out.ap()]), sem=s_cc)
            p.barrier()
            with p.scope():
                ib = layer * 164
                o_view = g2in.ap()[0:4096, :].rearrange("(hd r) c -> hd (r c)", r=4)
                if layer == 0:
                    steps = iter(make_dequant_steps(p, wspecs, "f1d"))
                    emit_attn(p, None, None, None, None, mk_ap, o_view, browd,
                              zrowd, "at", 0,
                              gath=(g1out.ap(), idx_ap[:, ib:ib + 132], 0),
                              filler=steps, fill_per=8)
                    for f in steps:
                        f()
                else:
                    emit_attn(p, None, None, None, None, mk_ap, o_view, browd,
                              zrowd, "at", 1, alt_ap=alt_ap, qsl_ap=qsl_ap,
                              ones2_ap=on2_ap,
                              gath=(g1out.ap(), idx_ap[:, ib:ib + 132], 0))
            p.barrier()
            s_cc2 = p.sem("cc2")
            p.op("gpsimd", lambda E: E.collective_compute(
                "AllGather", ALU.bypass, replica_groups=RG,
                ins=[g2in.ap()], outs=[g2out.ap()]), sem=s_cc2)
            p.barrier()
            with p.scope():
                ib = layer * 164 + 132
                emit_linres(p, None, wo if layer == 0 else wp, xcur.ap(),
                            xnext.ap(), "pr",
                            bias_ap=(None if layer == 0 else bp),
                            act_gath=(g2out.ap(), idx_ap[:, ib:ib + 32]))
            p.barrier()
            with p.scope():
                if layer == 0:
                    emit_ffn0(p, xnext.ap(), w0t_ap, alpha_ap, bias_ap, XA.ap(),
                              sw0.ap(), rowbuf, "f0")
                else:
                    emit_ffn1(p, xnext.ap(), dqd, y_out, rowbuf, xg_d, hg_d, "f1")
            p.barrier()
            if layer == 0:
                with p.scope():
                    emit_ffn1(p, XA.ap(), dqd, XB.ap(), rowbuf, xg_d, hg_d, "f1")
                p.barrier()
    nc = p.build(emit)
    return nc


_PROGS = {}


def _prog(name, builder):
    if name not in _PROGS:
        _PROGS[name] = builder()
    return _PROGS[name]


FUSED = True


def kernel(**inputs):
    if FUSED:
        return kernel_fused(**inputs)
    return kernel_unfused(**inputs)


def kernel_fused(**inputs):
    f32 = np.float32
    x = np.asarray(inputs["L0_attn_hidden_states"], dtype=f32)
    shards = shard_tokens_fm(x)
    w0t = np.ascontiguousarray(np.asarray(inputs["L0_ffn_weight"], f32).T)
    alpha_cm = chunkmajor(np.asarray(inputs["L0_ffn_alpha"], f32).reshape(-1))
    bias_cm = chunkmajor(np.asarray(inputs["L0_ffn_bias"], f32))
    ba_cm = np.ascontiguousarray(
        np.asarray(inputs["L1_attn_b_attn"], f32).reshape(96, 128).T)
    bp_cm = chunkmajor(np.asarray(inputs["L1_attn_b_proj"], f32))
    mk = host_masks()
    sc0 = 1.0 / math.sqrt(HD)
    nc = _prog("fused", _build_fused)
    im = []
    for c in range(NCORES):
        cq, sq_ = rot_tables(c, sc0)
        ck, sk_ = rot_tables(c, 1.0)
        alT, qsl, on2 = alibi_tables(c)
        im.append(dict(
            x=shards[c], w0t=w0t, alpha=alpha_cm, bias=bias_cm,
            wq=np.asarray(inputs["L0_attn_wq"], f32),
            wk=np.asarray(inputs["L0_attn_wk"], f32),
            wv=np.asarray(inputs["L0_attn_wv"], f32),
            wo=np.asarray(inputs["L0_attn_wo"], f32),
            wa=np.asarray(inputs["L1_attn_w_attn"], f32),
            ba=ba_cm,
            wp=np.asarray(inputs["L1_attn_w_proj"], f32),
            bp=bp_cm,
            cq=cq, sq=sq_, ck=ck, sk=sk_, mk=mk, alt=alT, qsl=qsl, on2=on2,
            idx=host_idx(c),
            gq=inputs["L1_ffn_gate_qweight"],
            gs=np.asarray(inputs["L1_ffn_gate_scales"], f32),
            gz=inputs["L1_ffn_gate_qzeros"],
            uq=inputs["L1_ffn_up_qweight"],
            us=np.asarray(inputs["L1_ffn_up_scales"], f32),
            uz=inputs["L1_ffn_up_qzeros"],
            dq=inputs["L1_ffn_down_qweight"],
            ds=np.asarray(inputs["L1_ffn_down_scales"], f32),
            dz=inputs["L1_ffn_down_qzeros"],
        ))
    res = _run(nc, im)
    return unshard_tokens_fm([res[c]["y"] for c in range(NCORES)]).astype(np.float32)


def kernel_unfused(**inputs):
    f32 = np.float32
    x = np.asarray(inputs["L0_attn_hidden_states"], dtype=f32)
    w0t = np.ascontiguousarray(np.asarray(inputs["L0_ffn_weight"], f32).T)
    alpha_cm = chunkmajor(np.asarray(inputs["L0_ffn_alpha"], f32).reshape(-1))
    bias_cm = chunkmajor(np.asarray(inputs["L0_ffn_bias"], f32))
    wq = np.asarray(inputs["L0_attn_wq"], f32)
    wk = np.asarray(inputs["L0_attn_wk"], f32)
    wv = np.asarray(inputs["L0_attn_wv"], f32)
    wo = np.asarray(inputs["L0_attn_wo"], f32)
    wa = np.asarray(inputs["L1_attn_w_attn"], f32)
    ba_cm = np.ascontiguousarray(
        np.asarray(inputs["L1_attn_b_attn"], f32).reshape(96, 128).T)
    wp = np.asarray(inputs["L1_attn_w_proj"], f32)
    bp_cm = chunkmajor(np.asarray(inputs["L1_attn_b_proj"], f32))
    zeros_cm = np.zeros((128, 32), f32)
    mk = host_masks()
    sc0 = 1.0 / math.sqrt(HD)

    x0 = shard_tokens_fm(x)

    nc = _prog("deq", _build_deq)
    im = [dict(gq=inputs["L1_ffn_gate_qweight"], gs=np.asarray(inputs["L1_ffn_gate_scales"], f32),
               gz=inputs["L1_ffn_gate_qzeros"],
               uq=inputs["L1_ffn_up_qweight"], us=np.asarray(inputs["L1_ffn_up_scales"], f32),
               uz=inputs["L1_ffn_up_qzeros"],
               dq=inputs["L1_ffn_down_qweight"], ds=np.asarray(inputs["L1_ffn_down_scales"], f32),
               dz=inputs["L1_ffn_down_qzeros"])] * NCORES
    rdq = _run(nc, im)

    def ffn0(xs):
        nc = _prog("ffn0", _build_ffn0)
        im = [dict(x=xs[c], w0t=w0t, alpha=alpha_cm, bias=bias_cm) for c in range(NCORES)]
        r = _run(nc, im)
        return [r[c]["y"] for c in range(NCORES)]

    def ffn1(xs):
        nc = _prog("ffn1", _build_ffn1)
        im = [dict(x=xs[c], dg=rdq[c]["dg"], du=rdq[c]["du"], dd=rdq[c]["dd"],
                   zg=rdq[c]["zg"], zu=rdq[c]["zu"], zd=rdq[c]["zd"])
              for c in range(NCORES)]
        r = _run(nc, im)
        return [r[c]["y"] for c in range(NCORES)]

    def attn_block(xs, layer):
        if layer == 0:
            nc = _prog("qkv0", lambda: _build_qkv(0))
            im = []
            for c in range(NCORES):
                cq, sq_ = rot_tables(c, sc0)
                ck, sk_ = rot_tables(c, 1.0)
                im.append(dict(x=xs[c], wq=wq, wk=wk, wv=wv, cq=cq, sq=sq_, ck=ck, sk=sk_))
        else:
            nc = _prog("qkv1", lambda: _build_qkv(1))
            im = [dict(x=xs[c], wa=wa, ba=ba_cm) for c in range(NCORES)]
        r = _run(nc, im)
        hs = _to_headshard(r)
        if layer == 0:
            nc = _prog("attn0", lambda: _build_attn(0))
            im = [dict(mk=mk, **hs[c]) for c in range(NCORES)]
        else:
            nc = _prog("attn1", lambda: _build_attn(1))
            im = []
            for c in range(NCORES):
                alT, qsl, on2 = alibi_tables(c)
                im.append(dict(mk=mk, alt=alT, qsl=qsl, on2=on2, **hs[c]))
        r = _run(nc, im)
        afm = _from_headshard(r)
        nc = _prog("proj", _build_proj)
        wmat = wo if layer == 0 else wp
        bmat = zeros_cm if layer == 0 else bp_cm
        im = [dict(a=afm[c], w=wmat, x=xs[c], b=bmat) for c in range(NCORES)]
        r = _run(nc, im)
        return [r[c]["y"] for c in range(NCORES)]

    x1 = ffn0(x0)
    x2 = attn_block(x1, 0)
    x3 = ffn0(x2)
    x4 = ffn1(x3)
    x5 = attn_block(x4, 1)
    x6 = ffn1(x5)
    global LAST_STAGES
    LAST_STAGES = [x1, x2, x3, x4, x5, x6]
    return unshard_tokens_fm(x6).astype(np.float32)


LAST_STAGES = None
